# version 10
# baseline (speedup 1.0000x reference)
import numpy as np
from contextlib import ExitStack
import concourse.bass as bass
import concourse.mybir as mybir
from concourse.bass_utils import run_bass_kernel_spmd
from concourse.replica_groups import maybe_share_collective_output_space

F32 = mybir.dt.float32
F32R = mybir.dt.float32r
BF16 = mybir.dt.bfloat16
AF = mybir.ActivationFunctionType
ALU = mybir.AluOpType
AX = mybir.AxisListType

P = 128
T = 16384
TC = 2048
TH = 1024
H = 1024
DSTATE = 16
DTR = 32
E = 8
I = 2048
TB = 1024
NB = T // TB
NCORES = 8
HC = H // P
IT = I // P
RG = [list(range(NCORES))]


_DEBUG_DIRS = (0, 1)
_DEBUG_DUMP = "r"
_DEBUG_TAP = None


class Ctr:
    def __init__(self, sem, step=1):
        self.sem, self.v, self.step = sem, 0, step

    def inc(self, inst, n=None):
        n = self.step if n is None else n
        inst.then_inc(self.sem, n)
        self.v += n
        return self.v


def build_program(stage=2):
    nc = bass.Bass("TRN2", target_bir_lowering=False, debug=False)

    xk_tok_h = nc.dram_tensor("xk_tok", [H, TC], F32, kind="ExternalInput").ap()
    xk_str_h = nc.dram_tensor("xk_str", [P, T], F32, kind="ExternalInput").ap()
    w1_h = nc.dram_tensor("w1", [H, P], F32, kind="ExternalInput").ap()
    dtw_h = nc.dram_tensor("dtw", [DTR, 2 * P], F32, kind="ExternalInput").ap()
    dtb_h = nc.dram_tensor("dtb", [P, 2], F32, kind="ExternalInput").ap()
    alog_h = nc.dram_tensor("alog", [P, 2 * DSTATE], F32, kind="ExternalInput").ap()
    dvec_h = nc.dram_tensor("dvec", [P, 2], F32, kind="ExternalInput").ap()
    gw_h = nc.dram_tensor("gw", [H, E], F32, kind="ExternalInput").ap()
    shg_h = nc.dram_tensor("shg", [IT, P, HC, P], F32, kind="ExternalInput").ap()
    shu_h = nc.dram_tensor("shu", [IT, P, HC, P], F32, kind="ExternalInput").ap()
    shd_h = nc.dram_tensor("shd", [HC, P, IT, P], F32, kind="ExternalInput").ap()
    rtg_h = nc.dram_tensor("rtg", [E, IT, P, HC, P], F32, kind="ExternalInput").ap()
    rtu_h = nc.dram_tensor("rtu", [E, IT, P, HC, P], F32, kind="ExternalInput").ap()
    rtd_h = nc.dram_tensor("rtd", [E, HC, P, IT, P], F32, kind="ExternalInput").ap()

    if stage >= 2:
        out_h = nc.dram_tensor("out", [TC, H], F32, kind="ExternalOutput").ap()
        dbg_h = (nc.dram_tensor("dbg", [TC, H], F32, kind="ExternalOutput").ap()
                 if _DEBUG_TAP else None)
    if stage == 1:
        rdump_h = nc.dram_tensor("rdump", [P, T], F32, kind="ExternalOutput").ap()

    dbc_loc = nc.dram_tensor("dbc_loc", [P, TC], F32, kind="Internal").ap()
    ag_space = maybe_share_collective_output_space("AllGather", RG)
    dbc_ag = nc.dram_tensor("dbc_ag", [NCORES * P, TC], F32, kind="Internal",
                            addr_space=ag_space).ap()
    r_loc = nc.dram_tensor("r_loc", [P, T], F32, kind="Internal").ap()
    rag_out = nc.dram_tensor("rag_out", [NCORES * P, T], F32, kind="Internal",
                             addr_space=ag_space).ap()

    ctx = ExitStack()
    with ctx:
        x_str = ctx.enter_context(nc.sbuf_tensor("x_str", [P, T], F32))
        y_sb = ctx.enter_context(nc.sbuf_tensor("y_sb", [P, T], F32))
        chain = ctx.enter_context(nc.sbuf_tensor("chain", [P, 2 * DSTATE], F32))
        ident = ctx.enter_context(nc.sbuf_tensor("ident", [P, P], F32))
        selb16 = ctx.enter_context(nc.sbuf_tensor("selb16", [80, DSTATE * P], F32R))
        selb8 = ctx.enter_context(nc.sbuf_tensor("selb8", [E, E * P], F32R))
        dtwr = ctx.enter_context(nc.sbuf_tensor("dtwr", [DTR, 2 * P], F32R))
        dtb_sb = ctx.enter_context(nc.sbuf_tensor("dtb_sb", [P, 2], F32))
        acol = ctx.enter_context(nc.sbuf_tensor("acol", [P, 2 * DSTATE], F32))
        dsum = ctx.enter_context(nc.sbuf_tensor("dsum", [P, 2], F32))
        gw_bf = ctx.enter_context(nc.sbuf_tensor("gw_bf", [P, HC * E], BF16))

        sems = {}
        for name in ["s_ld", "s_dbcd", "s_mm", "s_sc", "s_v", "s_g", "s_coll",
                     "s_st", "s_wd", "s_ffv", "s_ffs", "s_ffm",
                     "s_xc0", "s_xc1", "s_xstr", "s_rl", "s_r0", "s_r1",
                     "s_w0", "s_w1", "s_w2", "s_w3", "s_w4", "s_w5"]:
            sems[name] = ctx.enter_context(nc.semaphore(name))
        S_LD, S_DBCD, S_MM, S_SC, S_V, S_G = (sems[k] for k in
                                              ["s_ld", "s_dbcd", "s_mm", "s_sc", "s_v", "s_g"])
        S_COLL, S_ST, S_WD, S_FFV, S_FFS, S_FFM = (sems[k] for k in
                                                   ["s_coll", "s_st", "s_wd", "s_ffv", "s_ffs", "s_ffm"])
        block = ctx.enter_context(nc.Block())

        ld = Ctr(S_LD, 16)
        dbcd = Ctr(S_DBCD, 16)
        mm = Ctr(S_MM)
        sc = Ctr(S_SC)
        vv = Ctr(S_V)
        gg = Ctr(S_G)
        coll = Ctr(S_COLL)
        st = Ctr(S_ST, 16)

        stackA = ExitStack()
        xc_sb = stackA.enter_context(nc.sbuf_tensor("xc_sb", [P, 2 * TC], F32))
        xc_r = stackA.enter_context(nc.sbuf_tensor("xc_r", [P, 2 * TC], F32R))
        w1r = stackA.enter_context(nc.sbuf_tensor("w1r", [P, HC * P], F32R))
        w1f = stackA.enter_context(nc.sbuf_tensor("w1f", [P, HC * P], F32))
        dtwf = stackA.enter_context(nc.sbuf_tensor("dtwf", [DTR, 2 * P], F32))
        sel_f = stackA.enter_context(nc.sbuf_tensor("sel_f", [80, DSTATE * P], F32))
        ps_dbc = stackA.enter_context(nc.psum_tensor("ps_dbc", [P, TC], F32))

        @block.sync
        def _(sync):
            sync.dma_start(out=x_str[:], in_=xk_str_h[:]).then_inc(sems["s_xstr"], 16)
            ld.inc(sync.dma_start(out=w1f.ap().rearrange("p (c m) -> p c m", c=HC),
                                  in_=w1_h.rearrange("(c p) m -> p c m", p=P)))
            ld.inc(sync.dma_start(out=dtwf[:], in_=dtw_h[:]))
            ld.inc(sync.dma_start(out=dtb_sb[:], in_=dtb_h[:]))
            ld.inc(sync.dma_start(out=acol[:], in_=alog_h[:]))
            ld.inc(sync.dma_start(out=dsum[:], in_=dvec_h[:]))
            sync.wait_ge(S_LD, 16 * 5)
            for c in range(HC):
                if c >= 2:
                    sync.wait_ge(S_MM, c - 1)
                sync.dma_start(
                    out=xc_sb[:, (c % 2) * TC:(c % 2 + 1) * TC],
                    in_=xk_tok_h[c * P:(c + 1) * P, :]).then_inc(
                        sems["s_xc0" if c % 2 == 0 else "s_xc1"], 16)
        LD_ALL = ld.v

        @block.gpsimd
        def _(g):
            g.memset(ident[:], 0.0)
            g.affine_select(out=ident[:], in_=ident[:], compare_op=ALU.not_equal,
                            fill=1.0, base=0, pattern=[[-1, P]], channel_multiplier=1)
            g.memset(sel_f[:], 0.0)
            for rb in (32, 64):
                g.affine_select(out=sel_f[rb:rb + DSTATE, :], in_=sel_f[rb:rb + DSTATE, :],
                                compare_op=ALU.not_equal, fill=1.0, base=0,
                                pattern=[[1, DSTATE], [0, P]], channel_multiplier=-1)
                g.tensor_copy(out=selb16[rb:rb + DSTATE, :], in_=sel_f[rb:rb + DSTATE, :])
            g.memset(sel_f[0:E, 0:E * P], 0.0)
            g.affine_select(out=sel_f[0:E, 0:E * P], in_=sel_f[0:E, 0:E * P],
                            compare_op=ALU.not_equal, fill=1.0, base=0,
                            pattern=[[1, E], [0, P]], channel_multiplier=-1)
            g.tensor_copy(out=selb8[:], in_=sel_f[0:E, 0:E * P])
            gg.inc(g.memset(y_sb[:], 0.0))
        GG_CONST = gg.v

        @block.scalar
        def _(s):
            s.wait_ge(S_LD, 16 * 5)
            sc.inc(s.activation(out=acol[:], in_=acol[:], func=AF.Exp))
            s.wait_ge(S_SC, sc.v)
            sc.inc(s.mul(out=acol[:], in_=acol[:], mul=-1.0))
        SC_ACOL = sc.v

        @block.vector
        def _(v):
            v.wait_ge(S_LD, 16 * 5)
            vv.inc(v.tensor_tensor(out=dsum[:, 0:1], in0=dsum[:, 0:1], in1=dsum[:, 1:2], op=ALU.add))
            v.wait_ge(S_V, vv.v)
            vv.inc(v.tensor_scalar_add(out=dsum[:, 0:1], in0=dsum[:, 0:1], scalar1=1.0))
            v.wait_ge(S_V, vv.v)
            vv.inc(v.tensor_copy(out=w1r[:], in_=w1f[:]))
            vv.inc(v.tensor_copy(out=dtwr[:], in_=dtwf[:]))
        VV_SMALL = vv.v

        for c in range(HC):
            @block.vector
            def _(v, c=c):
                v.wait_ge(sems["s_xc0" if c % 2 == 0 else "s_xc1"], 16 * (c // 2 + 1))
                vv.inc(v.tensor_copy(out=xc_r[:, (c % 2) * TC:(c % 2 + 1) * TC],
                                     in_=xc_sb[:, (c % 2) * TC:(c % 2 + 1) * TC]))

            @block.tensor
            def _(t, c=c, vt=vv.v):
                t.wait_ge(S_V, vt)
                last = None
                for nk in range(TC // 512):
                    last = t.matmul(
                        out=ps_dbc[:, nk * 512:(nk + 1) * 512],
                        lhsT=w1r[:, c * P:(c + 1) * P],
                        rhs=xc_r[:, (c % 2) * TC + nk * 512:(c % 2) * TC + (nk + 1) * 512],
                        start=(c == 0), stop=(c == HC - 1))
                mm.inc(last)
        MM_DBC = mm.v

        @block.vector
        def _(v):
            v.wait_ge(S_MM, MM_DBC)
            vv.inc(v.tensor_copy(out=xc_sb[:, 0:TC], in_=ps_dbc[:]))
        VV_DBCE = vv.v

        @block.sync
        def _(sync):
            sync.wait_ge(S_V, VV_DBCE)
            st.inc(sync.dma_start(out=dbc_loc[:], in_=xc_sb[:, 0:TC]))
        ST_DBC = st.v

        @block.gpsimd
        def _(g):
            g.wait_ge(S_ST, ST_DBC)
            coll.inc(g.collective_compute("AllGather", ALU.bypass, replica_groups=RG,
                                          ins=[dbc_loc], outs=[dbc_ag]))
        COLL_AG = coll.v

        stackA.close()

        stackB = ExitStack()
        dbc_sb = stackB.enter_context(nc.sbuf_tensor("dbc_sb", [96, TB], F32))
        dbc_r = stackB.enter_context(nc.sbuf_tensor("dbc_r", [96, 2 * TB], F32R))
        spu = stackB.enter_context(nc.sbuf_tensor("spu", [P, TB], F32))
        delta = stackB.enter_context(nc.sbuf_tensor("delta", [P, TB], F32))
        a_sb = stackB.enter_context(nc.sbuf_tensor("a_sb", [P, 2 * TB], F32))
        bx_sb = stackB.enter_context(nc.sbuf_tensor("bx_sb", [P, TB], F32))
        h_sb = stackB.enter_context(nc.sbuf_tensor("h_sb", [P, 2 * TB], F32))
        hc_sb = stackB.enter_context(nc.sbuf_tensor("hc_sb", [P, 2 * TB], F32))
        bbc_sb = stackB.enter_context(nc.sbuf_tensor("bbc_sb", [P, 2 * TB], F32))
        cbc_sb = stackB.enter_context(nc.sbuf_tensor("cbc_sb", [P, 2 * TB], F32))
        pool = [stackB.enter_context(nc.psum_tensor(f"pp{i}", [P, TB], F32))
                for i in range(4)]
        pool_guard = [0, 0, 0, 0]
        pool_i = [0]

        def next_slot():
            s_ = pool_i[0] % 4
            pool_i[0] += 1
            return s_

        dbc_ag_v = dbc_ag.rearrange("(r p) t -> r p t", p=P)
        blk_order = []
        if 0 in _DEBUG_DIRS:
            blk_order += [(0, bi) for bi in range(NB)]
        if 1 in _DEBUG_DIRS:
            blk_order += [(1, bi) for bi in reversed(range(NB))]

        v_round = {}
        v_bx = {}
        v_scan = {}
        v_yadd = {}
        g_iter = {}
        sc_iter = {}
        mm_end = {}
        sc_sp_prev = None
        v_u_prev = None
        NS = DSTATE

        def gs_of(idx, s):
            return idx * NS + s

        def by_gs(gs):
            return (gs // NS, gs % NS)

        for idx, (dr, bi) in enumerate(blk_order):
            pb = idx % 2
            rowbase = 64 * dr
            sender = bi // 2
            col0 = (bi % 2) * TB

            @block.sync
            def _(sync, pb=pb, rowbase=rowbase, sender=sender, col0=col0, idx=idx):
                if idx == 0:
                    sync.wait_ge(S_COLL, COLL_AG)
                if idx >= 1:
                    sync.wait_ge(S_V, v_round[idx - 1])
                dbcd.inc(sync.dma_start(
                    out=dbc_sb[0:DTR, :],
                    in_=dbc_ag_v[sender, rowbase:rowbase + DTR, col0:col0 + TB]))
                dbcd.inc(sync.dma_start(
                    out=dbc_sb[32:32 + DSTATE, :],
                    in_=dbc_ag_v[sender, rowbase + 32:rowbase + 48, col0:col0 + TB]))
                dbcd.inc(sync.dma_start(
                    out=dbc_sb[64:64 + DSTATE, :],
                    in_=dbc_ag_v[sender, rowbase + 48:rowbase + 64, col0:col0 + TB]))
            DBCD_T = dbcd.v

            @block.vector
            def _(v, pb=pb, idx=idx):
                v.wait_ge(S_DBCD, DBCD_T)
                if idx >= 2:
                    v.wait_ge(S_MM, mm_end[idx - 2])
                o = pb * TB
                v.tensor_copy(out=dbc_r[0:DTR, o:o + TB], in_=dbc_sb[0:DTR, :])
                v.tensor_copy(out=dbc_r[32:48, o:o + TB], in_=dbc_sb[32:48, :])
                vv.inc(v.tensor_copy(out=dbc_r[64:80, o:o + TB], in_=dbc_sb[64:80, :]))
            v_round[idx] = vv.v

            dslot = next_slot()
            @block.tensor
            def _(t, dr=dr, pb=pb, vt=vv.v, idx=idx, dslot=dslot, gd=pool_guard[dslot]):
                t.wait_ge(S_V, vt)
                if idx == 0:
                    t.wait_ge(S_SC, SC_ACOL)
                if gd:
                    t.wait_ge(S_SC, gd)
                last = None
                for nk in range(TB // 512):
                    last = t.matmul(out=pool[dslot][:, nk * 512:(nk + 1) * 512],
                                    lhsT=dtwr[:, dr * P:(dr + 1) * P],
                                    rhs=dbc_r[0:DTR, pb * TB + nk * 512:pb * TB + (nk + 1) * 512],
                                    start=True, stop=True)
                mm.inc(last)
            MM_D = mm.v

            @block.scalar
            def _(s, dr=dr, idx=idx, dslot=dslot):
                s.wait_ge(S_MM, MM_D)
                if idx >= 1:
                    s.wait_ge(S_V, v_bx[gs_of(idx - 1, NS - 1)])
                s.activation(out=spu[:], in_=pool[dslot][:], func=AF.Exp,
                             bias=dtb_sb[:, dr:dr + 1])
                sc.inc(s.activation(out=delta[:], in_=spu[:], func=AF.Ln, bias=1.0))
            sc_sp_prev = sc.v
            pool_guard[dslot] = sc.v

            @block.vector
            def _(v, bi=bi, idx=idx, sct=sc.v):
                v.wait_ge(S_SC, sct)
                if idx == 0:
                    v.wait_ge(sems["s_xstr"], 16)
                vv.inc(v.tensor_tensor(out=spu[:], in0=delta[:],
                                       in1=x_str[:, bi * TB:(bi + 1) * TB], op=ALU.mult))
            v_u_prev = vv.v

            for s in range(NS):
                sb = s % 2
                gs = gs_of(idx, s)
                sb2 = gs % 2

                bslot = next_slot()
                cslot = next_slot()
                @block.tensor
                def _(t, s=s, pb=pb, bslot=bslot, cslot=cslot,
                      gb=pool_guard[bslot], gc=pool_guard[cslot], vt=v_round[idx]):
                    t.wait_ge(S_V, vt)
                    if gb:
                        t.wait_ge(S_SC, gb)
                    if gc:
                        t.wait_ge(S_SC, gc)
                    last = None
                    for nk in range(TB // 512):
                        t.matmul(out=pool[bslot][:, nk * 512:(nk + 1) * 512],
                                 lhsT=selb16[32:48, s * P:(s + 1) * P],
                                 rhs=dbc_r[32:48, pb * TB + nk * 512:pb * TB + (nk + 1) * 512],
                                 start=True, stop=True)
                        last = t.matmul(out=pool[cslot][:, nk * 512:(nk + 1) * 512],
                                        lhsT=selb16[64:80, s * P:(s + 1) * P],
                                        rhs=dbc_r[64:80, pb * TB + nk * 512:pb * TB + (nk + 1) * 512],
                                        start=True, stop=True)
                    mm.inc(last)
                MM_BC = mm.v

                @block.scalar
                def _(s_, s=s, sb=sb, dr=dr, gs=gs, bslot=bslot, cslot=cslot):
                    s_.wait_ge(S_MM, MM_BC)
                    if gs >= 2:
                        pgs = gs - 2
                        s_.wait_ge(S_V, v_bx[pgs])
                        s_.wait_ge(S_G, g_iter[pgs])
                        s_.wait_ge(S_V, v_scan[pgs])
                    s_.copy(out=bbc_sb[:, sb * TB:(sb + 1) * TB], in_=pool[bslot][:])
                    s_.copy(out=cbc_sb[:, sb * TB:(sb + 1) * TB], in_=pool[cslot][:])
                    sc.inc(s_.activation(out=a_sb[:, sb * TB:(sb + 1) * TB],
                                         in_=delta[:], func=AF.Exp,
                                         scale=acol[:, dr * DSTATE + s:dr * DSTATE + s + 1]))
                sc_iter[gs] = sc.v
                pool_guard[bslot] = sc.v
                pool_guard[cslot] = sc.v

                @block.vector
                def _(v, s=s, sb=sb, sb2=sb2, dr=dr, bi=bi, idx=idx, gs=gs, sct=sc.v):
                    v.wait_ge(S_SC, sct)
                    vv.inc(v.tensor_tensor(out=bx_sb[:], in0=spu[:],
                                           in1=bbc_sb[:, sb * TB:(sb + 1) * TB], op=ALU.mult))
                    v_bx[gs] = vv.v
                    v.wait_ge(S_V, vv.v)
                    if gs >= 2:
                        v.wait_ge(S_G, g_iter[gs - 2])
                    first = (idx == 0) or (dr == 1 and blk_order[idx - 1][0] == 0)
                    ci = dr * DSTATE + s
                    init = 0.0 if first else chain[:, ci:ci + 1]
                    ho = sb2 * TB
                    if dr == 0:
                        ii = v.tensor_tensor_scan(out=h_sb[:, ho:ho + TB],
                                                  data0=a_sb[:, sb * TB:(sb + 1) * TB],
                                                  data1=bx_sb[:], initial=init,
                                                  op0=ALU.mult, op1=ALU.add)
                    else:
                        ii = v.tensor_tensor_scan(out=h_sb[:, ho:ho + TB],
                                                  data0=a_sb[:, sb * TB:(sb + 1) * TB][:, ::-1],
                                                  data1=bx_sb[:, ::-1], initial=init,
                                                  op0=ALU.mult, op1=ALU.add)
                    vv.inc(ii)
                    v_scan[gs] = vv.v

                @block.gpsimd
                def _(g, sb=sb, sb2=sb2, dr=dr, bi=bi, s=s, gs=gs, vt=v_scan[gs], sct=sc.v):
                    g.wait_ge(S_V, vt)
                    g.wait_ge(S_SC, sct)
                    ho = sb2 * TB
                    if dr == 0:
                        gg.inc(g.tensor_tensor(out=hc_sb[:, ho:ho + TB],
                                               in0=h_sb[:, ho:ho + TB],
                                               in1=cbc_sb[:, sb * TB:(sb + 1) * TB],
                                               op=ALU.mult))
                    else:
                        gg.inc(g.tensor_tensor(out=hc_sb[:, ho:ho + TB],
                                               in0=h_sb[:, ho:ho + TB],
                                               in1=cbc_sb[:, sb * TB:(sb + 1) * TB][:, ::-1],
                                               op=ALU.mult))
                    g.wait_ge(S_G, gg.v)
                    ci = dr * DSTATE + s
                    g.tensor_copy(out=chain[:, ci:ci + 1],
                                  in_=h_sb[:, ho + TB - 1:ho + TB])
                    src = hc_sb[:, ho:ho + TB]
                    if dr == 1:
                        src = src[:, ::-1]
                    gg.inc(g.tensor_tensor(out=y_sb[:, bi * TB:(bi + 1) * TB],
                                           in0=y_sb[:, bi * TB:(bi + 1) * TB],
                                           in1=src, op=ALU.add))
                g_iter[gs] = gg.v
                v_yadd[gs] = gg.v

            mm_end[idx] = mm.v

        r_st = []
        for j in range(NCORES):
            rbuf = a_sb if j % 2 == 0 else cbc_sb

            @block.vector
            def _(v, j=j, rbuf=rbuf):
                if j == 0:
                    lastgs = gs_of(len(blk_order) - 1, NS - 1)
                    v.wait_ge(S_G, g_iter[lastgs])
                if j >= 2:
                    v.wait_ge(sems["s_r0" if j % 2 == 0 else "s_r1"], 16 * (j // 2))
                ii = None
                for q in range(2):
                    blk = slice(j * TC + q * TB, j * TC + (q + 1) * TB)
                    ii = v.scalar_tensor_tensor(out=rbuf[:, q * TB:(q + 1) * TB],
                                                in0=x_str[:, blk], scalar=dsum[:, 0:1],
                                                in1=y_sb[:, blk], op0=ALU.mult, op1=ALU.add)
                vv.inc(ii)

            @block.sync
            def _(sync, j=j, rbuf=rbuf, vt=vv.v):
                sync.wait_ge(S_V, vt)
                if stage == 1:
                    src = rbuf[:] if _DEBUG_DUMP == "r" else y_sb[:, j * TC:(j + 1) * TC]
                    st.inc(sync.dma_start(out=rdump_h[:, j * TC:(j + 1) * TC], in_=src))
                sync.dma_start(out=r_loc[:, j * TC:(j + 1) * TC],
                               in_=rbuf[:]).then_inc(
                                   sems["s_r0" if j % 2 == 0 else "s_r1"], 16)
            r_st.append(st.v)

        @block.gpsimd
        def _(g, stv=st.v):
            g.wait_ge(S_ST, stv)
            g.wait_ge(sems["s_r0"], 16 * 4)
            g.wait_ge(sems["s_r1"], 16 * 4)
            coll.inc(g.collective_compute("AllGather", ALU.bypass, replica_groups=RG,
                                          ins=[r_loc], outs=[rag_out]))
        COLL_A2A = coll.v

        stackB.close()

        if stage == 1:
            @block.sync
            def _(sync):
                sync.wait_ge(S_COLL, COLL_A2A)
                sync.wait_ge(S_ST, st.v)
            return nc

        stackD = ExitStack()
        wld = stackD.enter_context(nc.sbuf_tensor("wld", [P, 6 * 1024], F32))
        wbf = stackD.enter_context(nc.sbuf_tensor("wbf", [P, 6 * 1024], BF16))
        sact = stackD.enter_context(nc.sbuf_tensor("sact", [P, TH], F32))
        stat = stackD.enter_context(nc.sbuf_tensor("stat", [P, 16], F32))
        lgsm = stackD.enter_context(nc.sbuf_tensor("lgsm", [E, TH], F32))
        sm_fm_t = stackD.enter_context(nc.sbuf_tensor("sm_fm_t", [E, TH], F32R))
        lgtm = stackD.enter_context(nc.sbuf_tensor("lgtm", [P, 8 * E], F32))
        smtm = stackD.enter_context(nc.sbuf_tensor("smtm", [P, 8 * E], F32))
        sm_bf = stackD.enter_context(nc.sbuf_tensor("sm_bf", [P, TH], BF16))
        selb8d = stackD.enter_context(nc.sbuf_tensor("selb8d", [E, E * P], F32R))
        ps_gu = [stackD.enter_context(nc.psum_tensor(f"ps_gu{i}", [P, 512], F32))
                 for i in range(4)]
        ps_y = stackD.enter_context(nc.psum_tensor("ps_y", [P, TH], F32))
        ps_ty = stackD.enter_context(nc.psum_tensor("ps_ty", [P, TH], F32))

        r_fm = x_str[:, 0:8 * TH]
        x1_tm = x_str[:, 8 * TH:16 * TH]
        x1_bf = x_str[:, 0:4 * TH].bitcast(BF16)
        xs_bf = x_str[:, 4 * TH:8 * TH].bitcast(BF16)
        y_fm = y_sb[:, 0:8 * TH]
        act_bf = y_sb[:, 8 * TH:16 * TH].bitcast(BF16)
        sm_fm = sm_fm_t.ap()
        lg_sb = lgsm.ap()

        fm = Ctr(S_FFM)
        fv = Ctr(S_FFV)
        fs = Ctr(S_FFS)
        wd = Ctr(S_WD, 16)

        def pst_of(j):
            return ps_ty[:, (j % 2) * 512:(j % 2) * 512 + P]

        @block.gpsimd
        def _(g):
            g.wait_ge(S_COLL, COLL_A2A)
            g.memset(sact[0:E, 0:E * P], 0.0)
            g.affine_select(out=sact[0:E, 0:E * P], in_=sact[0:E, 0:E * P],
                            compare_op=ALU.not_equal, fill=1.0, base=0,
                            pattern=[[1, E], [0, P]], channel_multiplier=-1)
            gg.inc(g.tensor_copy(out=selb8d[:], in_=sact[0:E, 0:E * P]))
        GG_SELB8 = gg.v

        @block.sync
        def _(sync):
            sync.wait_ge(S_COLL, COLL_A2A)
            wd.inc(sync.dma_start(out=wld[:, 0:HC * E].rearrange("p (c m) -> p c m", c=HC),
                                  in_=gw_h.rearrange("(c p) m -> p c m", p=P)))
        @block.vector
        def _(v, wt=wd.v):
            v.wait_ge(S_WD, wt)
            fv.inc(v.tensor_copy(out=gw_bf[:], in_=wld[:, 0:HC * E]))
        FV_GW = fv.v

        rag_v = rag_out.rearrange("(c p) t -> p c t", p=P)

        NGRAN = 6
        wsems = [sems[f"s_w{i}"] for i in range(NGRAN)]
        wld_free = [0] * NGRAN
        wld_free_sem = [None] * NGRAN
        wld_free_sem[0] = S_FFV
        wld_free[0] = FV_GW
        wld_dma = [(None, 0)] * NGRAN
        wld_uses = [0] * NGRAN
        wbf_free = [0] * NGRAN
        ring = {"w": 0, "b": 0}

        def emit_wload(src_ap, ngran):
            while True:
                cand = ring["w"] % NGRAN
                if ngran == 2 and cand % 2 == 1:
                    ring["w"] += 1
                    continue
                g0 = cand
                ring["w"] += ngran
                break
            waits = [(wld_free_sem[g0 + k], wld_free[g0 + k]) for k in range(ngran)]
            wld_uses[g0] += 1
            @block.sync
            def _(sync, g0=g0, src_ap=src_ap, ngran=ngran, waits=waits):
                for sem_, wt in waits:
                    if wt:
                        sync.wait_ge(sem_, wt)
                sync.dma_start(
                    out=wld[:, g0 * 1024:(g0 + ngran) * 1024].rearrange(
                        "p (c m) -> p c m", m=P),
                    in_=src_ap).then_inc(wsems[g0], 16)
            mark = (wsems[g0], 16 * wld_uses[g0])
            for k in range(ngran):
                wld_dma[g0 + k] = mark
            return g0, mark

        def emit_wcast(g0, wt, ngran, eng):
            while True:
                cand = ring["b"] % NGRAN
                if ngran == 2 and cand % 2 == 1:
                    ring["b"] += 1
                    continue
                b0 = cand
                ring["b"] += ngran
                break
            waits = [wbf_free[b0 + k] for k in range(ngran)]

            def body(e_):
                e_.wait_ge(wt[0], wt[1])
                for w_ in waits:
                    if w_:
                        e_.wait_ge(S_FFM, w_)

            if eng == "s":
                @block.scalar
                def _(s):
                    body(s)
                    fs.inc(s.copy(out=wbf[:, b0 * 1024:(b0 + ngran) * 1024],
                                  in_=wld[:, g0 * 1024:(g0 + ngran) * 1024]))
                mark_sem, mark = S_FFS, fs.v
            else:
                @block.gpsimd
                def _(g):
                    body(g)
                    gg.inc(g.tensor_copy(out=wbf[:, b0 * 1024:(b0 + ngran) * 1024],
                                         in_=wld[:, g0 * 1024:(g0 + ngran) * 1024]))
                mark_sem, mark = S_G, gg.v
            for k in range(ngran):
                wld_free_sem[g0 + k] = mark_sem
                wld_free[g0 + k] = mark
            return b0, (mark_sem, mark)

        ST_PREV_HALF = 0
        fv_act = {}
        fs_silu = {}
        gu_step = [0]
        fv_yev_prev = 0
        fm_gu_prev = 0
        fs_sm_prev = 0

        for hf in range(2):
            hofs = hf * TH

            @block.sync
            def _(sync, hf=hf, stp=ST_PREV_HALF, fmv=fm.v):
                if hf == 0:
                    sync.wait_ge(S_COLL, COLL_A2A)
                else:
                    sync.wait_ge(S_ST, stp)
                    sync.wait_ge(S_FFM, fmv)
                pid = nc.partition_id([mybir.EngineType.SP])
                sync.dma_start(out=r_fm.rearrange("p (c t) -> p c t", c=HC),
                               in_=rag_v[:, :, bass.ts(pid * 2 + hf, TH)]).then_inc(
                                   sems["s_rl"], 16)
            ST_R = 16 * (hf + 1)

            v_tp = {}
            n_tp = 0
            for c in range(HC):
                for tt in range(8):
                    @block.tensor
                    def _(t, c=c, tt=tt, n_tp=n_tp, fsv=fs.v):
                        if n_tp == 0:
                            t.wait_ge(sems["s_rl"], ST_R)
                            t.wait_ge(S_FFS, fsv)
                        if n_tp >= 2:
                            t.wait_ge(S_FFV, v_tp[n_tp - 2])
                        fm.inc(t.transpose(out=pst_of(n_tp),
                                           in_=r_fm[:, c * TH + tt * P:c * TH + (tt + 1) * P],
                                           identity=ident[:]))
                    @block.vector
                    def _(v, c=c, tt=tt, n_tp=n_tp, mt=fm.v):
                        v.wait_ge(S_FFM, mt)
                        fv.inc(v.tensor_copy(
                            out=x1_tm[:, tt * H + c * P:tt * H + (c + 1) * P],
                            in_=pst_of(n_tp)))
                    v_tp[n_tp] = fv.v
                    n_tp += 1
            FV_RTM = fv.v

            def emit_ln(buf, tt, first_wait):
                @block.scalar
                def _(s, tt=tt, fw=first_wait):
                    if fw is not None:
                        s.wait_ge(S_FFV, fw)
                    fs.inc(s.activation(out=sact[:], in_=buf[:, tt * H:(tt + 1) * H],
                                        func=AF.Square))
                @block.vector
                def _(v, tt=tt, st_=fs.v):
                    v.wait_ge(S_FFS, st_)
                    v.reduce_sum(out=stat[:, 0:1], in_=buf[:, tt * H:(tt + 1) * H], axis=AX.X)
                    fv.inc(v.reduce_sum(out=stat[:, 1:2], in_=sact[:], axis=AX.X))
                    v.wait_ge(S_FFV, fv.v)
                    v.tensor_scalar_mul(out=stat[:, 0:1], in0=stat[:, 0:1], scalar1=1.0 / H)
                    fv.inc(v.tensor_scalar_mul(out=stat[:, 1:2], in0=stat[:, 1:2], scalar1=1.0 / H))
                    v.wait_ge(S_FFV, fv.v)
                    fv.inc(v.tensor_tensor(out=stat[:, 2:3], in0=stat[:, 0:1],
                                           in1=stat[:, 0:1], op=ALU.mult))
                    v.wait_ge(S_FFV, fv.v)
                    fv.inc(v.tensor_tensor(out=stat[:, 1:2], in0=stat[:, 1:2],
                                           in1=stat[:, 2:3], op=ALU.subtract))
                    v.wait_ge(S_FFV, fv.v)
                    fv.inc(v.tensor_scalar_add(out=stat[:, 1:2], in0=stat[:, 1:2],
                                               scalar1=1e-5))
                    v.wait_ge(S_FFV, fv.v)
                    fv.inc(v.reciprocal(out=stat[:, 2:3], in_=stat[:, 1:2]))
                @block.scalar
                def _(s, vt=fv.v):
                    s.wait_ge(S_FFV, vt)
                    fs.inc(s.sqrt(out=stat[:, 3:4], in_=stat[:, 2:3]))
                @block.vector
                def _(v, tt=tt, st_=fs.v):
                    v.wait_ge(S_FFS, st_)
                    fv.inc(v.tensor_scalar(out=buf[:, tt * H:(tt + 1) * H],
                                           in0=buf[:, tt * H:(tt + 1) * H],
                                           scalar1=stat[:, 0:1], scalar2=stat[:, 3:4],
                                           op0=ALU.subtract, op1=ALU.mult))
                return fv.v

            for tt in range(8):
                emit_ln(x1_tm, tt, FV_RTM if tt == 0 else None)
            FV_LN1 = fv.v
            if _DEBUG_TAP == "x1":
                @block.sync
                def _(sync, hf=hf, vt=fv.v):
                    sync.wait_ge(S_FFV, vt)
                    st.inc(sync.dma_start(
                        out=dbg_h.rearrange("(tt p) hh -> p tt hh", p=P)[
                            :, hf * 8:(hf + 1) * 8, :],
                        in_=x1_tm.rearrange("p (tt hh) -> p tt hh", tt=8)))

            v_tp2 = {}
            n_tp2 = 0
            for c in range(HC):
                for tt in range(8):
                    @block.tensor
                    def _(t, c=c, tt=tt, n_tp2=n_tp2):
                        if n_tp2 == 0:
                            t.wait_ge(S_FFV, FV_LN1)
                        if n_tp2 >= 2:
                            t.wait_ge(S_FFV, v_tp2[n_tp2 - 2])
                        fm.inc(t.transpose(out=pst_of(n_tp2),
                                           in_=x1_tm[:, tt * H + c * P:tt * H + (c + 1) * P],
                                           identity=ident[:]))
                    @block.vector
                    def _(v, c=c, tt=tt, n_tp2=n_tp2, mt=fm.v):
                        v.wait_ge(S_FFM, mt)
                        fv.inc(v.tensor_copy(
                            out=x1_bf[:, c * TH + tt * P:c * TH + (tt + 1) * P],
                            in_=pst_of(n_tp2)))
                    v_tp2[n_tp2] = fv.v
                    n_tp2 += 1
            FV_X1BF = fv.v

            @block.tensor
            def _(t, yp=fv_yev_prev):
                t.wait_ge(S_FFV, FV_X1BF)
                t.wait_ge(S_FFV, FV_GW)
                if yp:
                    t.wait_ge(S_FFV, yp)
                last = None
                for c in range(HC):
                    for nk in range(TH // 512):
                        last = t.matmul(out=ps_y[0:E, nk * 512:(nk + 1) * 512],
                                        lhsT=gw_bf[:, c * E:(c + 1) * E],
                                        rhs=x1_bf[:, c * TH + nk * 512:c * TH + (nk + 1) * 512],
                                        start=(c == 0), stop=(c == HC - 1))
                fm.inc(last)
            @block.vector
            def _(v, mt=fm.v):
                v.wait_ge(S_FFM, mt)
                fv.inc(v.tensor_copy(out=lg_sb[:], in_=ps_y[0:E, :]))
            FV_LG = fv.v

            v_lgt = {}
            for tt in range(8):
                @block.tensor
                def _(t, tt=tt):
                    if tt == 0:
                        t.wait_ge(S_FFV, FV_LG)
                    if tt >= 2:
                        t.wait_ge(S_FFV, v_lgt[tt - 2])
                    fm.inc(t.transpose(out=pst_of(tt)[0:P, 0:E],
                                       in_=lg_sb[:, tt * P:(tt + 1) * P],
                                       identity=ident[0:E, 0:E]))
                @block.vector
                def _(v, tt=tt, mt=fm.v):
                    v.wait_ge(S_FFM, mt)
                    fv.inc(v.tensor_copy(out=lgtm[:, tt * E:(tt + 1) * E],
                                         in_=pst_of(tt)[0:P, 0:E]))
                v_lgt[tt] = fv.v

            for tt in range(8):
                @block.vector
                def _(v, tt=tt, vt=v_lgt[7]):
                    if tt == 0:
                        v.wait_ge(S_FFV, vt)
                    fv.inc(v.reduce_max(out=stat[:, 4 + tt:5 + tt],
                                        in_=lgtm[:, tt * E:(tt + 1) * E], axis=AX.X))
                    v.wait_ge(S_FFV, fv.v)
                    fv.inc(v.tensor_single_scalar(out=smtm[:, tt * E:(tt + 1) * E],
                                                  in_=lgtm[:, tt * E:(tt + 1) * E],
                                                  scalar=stat[:, 4 + tt:5 + tt],
                                                  op=ALU.is_equal))
            FV_EQ = fv.v
            @block.scalar
            def _(s, vt=FV_EQ):
                s.wait_ge(S_FFV, vt)
                fs.inc(s.activation(out=stat[:, 4:12], in_=stat[:, 4:12], func=AF.Sigmoid))
            FS_SIG = fs.v
            for tt in range(8):
                @block.vector
                def _(v, tt=tt, st_=FS_SIG):
                    if tt == 0:
                        v.wait_ge(S_FFS, st_)
                    fv.inc(v.tensor_scalar_mul(out=smtm[:, tt * E:(tt + 1) * E],
                                               in0=smtm[:, tt * E:(tt + 1) * E],
                                               scalar1=stat[:, 4 + tt:5 + tt]))
            FV_SM = fv.v

            v_smt = {}
            for tt in range(8):
                @block.tensor
                def _(t, tt=tt):
                    if tt == 0:
                        t.wait_ge(S_FFV, FV_SM)
                    if tt >= 2:
                        t.wait_ge(S_FFV, v_smt[tt - 2])
                    fm.inc(t.transpose(out=pst_of(tt)[0:E, 0:P],
                                       in_=smtm[:, tt * E:(tt + 1) * E],
                                       identity=ident[:]))
                @block.vector
                def _(v, tt=tt, mt=fm.v):
                    v.wait_ge(S_FFM, mt)
                    fv.inc(v.tensor_copy(out=sm_fm[:, tt * P:(tt + 1) * P],
                                         in_=pst_of(tt)[0:E, 0:P]))
                v_smt[tt] = fv.v
            FV_SMFM = fv.v

            for ei in range(9):
                if ei == 0:
                    gsrc, usrc, dsrc = shg_h, shu_h, shd_h
                    rhs_t = x1_bf
                else:
                    gsrc, usrc, dsrc = rtg_h[ei - 1], rtu_h[ei - 1], rtd_h[ei - 1]
                    rhs_t = xs_bf

                if ei >= 1:
                    e = ei - 1
                    @block.tensor
                    def _(t, e=e, sp=fs_sm_prev, lastv=fv.v):
                        t.wait_ge(S_FFV, FV_SMFM)
                        t.wait_ge(S_G, GG_SELB8)
                        if sp:
                            t.wait_ge(S_FFS, sp)
                        else:
                            t.wait_ge(S_FFV, lastv)
                        last = None
                        for nk in range(TH // 512):
                            last = t.matmul(out=ps_ty[:, nk * 512:(nk + 1) * 512],
                                            lhsT=selb8d[:, e * P:(e + 1) * P],
                                            rhs=sm_fm[:, nk * 512:(nk + 1) * 512],
                                            start=True, stop=True)
                        fm.inc(last)
                    @block.scalar
                    def _(s, mt=fm.v):
                        s.wait_ge(S_FFM, mt)
                        fs.inc(s.copy(out=sm_bf[:], in_=ps_ty[:]))
                    fs_sm_prev = fs.v
                    for c in range(HC):
                        @block.vector
                        def _(v, c=c, st_=fs.v, e=e, gup=fm_gu_prev):
                            if c == 0:
                                v.wait_ge(S_FFS, st_)
                                if gup:
                                    v.wait_ge(S_FFM, gup)
                            fv.inc(v.tensor_tensor(out=xs_bf[:, c * TH:(c + 1) * TH],
                                                   in0=x1_bf[:, c * TH:(c + 1) * TH],
                                                   in1=sm_bf[:], op=ALU.mult))
                FV_XS = fv.v

                for it in range(IT):
                    gstep = gu_step[0]
                    gu_step[0] += 1
                    g0, wt0 = emit_wload(gsrc[it], 1)
                    bg, _ = emit_wcast(g0, wt0, 1, "s")
                    g1, wt1 = emit_wload(usrc[it], 1)
                    bu, (ctsem, ctv) = emit_wcast(g1, wt1, 1, "g")
                    @block.tensor
                    def _(t, it=it, bg=bg, bu=bu, ctsem=ctsem, ctv=ctv,
                          xs_t=FV_XS, rhs_t=rhs_t, fss=fs.v, gstep=gstep):
                        t.wait_ge(S_FFS, fss)
                        t.wait_ge(ctsem, ctv)
                        t.wait_ge(S_FFV, xs_t)
                        for nk in range(2):
                            sw = fs_silu.get((gstep - 1, nk), 0)
                            aw = fv_act.get((gstep - 1, nk), 0)
                            if sw:
                                t.wait_ge(S_FFS, sw)
                            if aw:
                                t.wait_ge(S_FFV, aw)
                        last = None
                        for c in range(HC):
                            for nk in range(2):
                                last = t.matmul(out=ps_gu[nk][:],
                                                lhsT=wbf[:, bg * 1024 + c * P:bg * 1024 + (c + 1) * P],
                                                rhs=rhs_t[:, c * TH + nk * 512:c * TH + (nk + 1) * 512],
                                                start=(c == 0), stop=(c == HC - 1))
                        fm.inc(last)
                        for c in range(HC):
                            for nk in range(2):
                                last = t.matmul(out=ps_gu[2 + nk][:],
                                                lhsT=wbf[:, bu * 1024 + c * P:bu * 1024 + (c + 1) * P],
                                                rhs=rhs_t[:, c * TH + nk * 512:c * TH + (nk + 1) * 512],
                                                start=(c == 0), stop=(c == HC - 1))
                        fm.inc(last)
                    FM_G = fm.v - 1
                    FM_GU = fm.v
                    wbf_free[bg] = fm.v
                    wbf_free[bu] = fm.v
                    for nk in range(2):
                        @block.scalar
                        def _(s, it=it, nk=nk, mt=FM_G):
                            s.wait_ge(S_FFM, mt)
                            fs.inc(s.activation(out=sact[:, nk * 512:(nk + 1) * 512],
                                                in_=ps_gu[nk][:], func=AF.Silu))
                        fs_silu[(gstep, nk)] = fs.v
                        @block.vector
                        def _(v, it=it, nk=nk, st_=fs.v, mt=FM_GU):
                            v.wait_ge(S_FFS, st_)
                            v.wait_ge(S_FFM, mt)
                            fv.inc(v.tensor_tensor(
                                out=act_bf[:, it * TH + nk * 512:it * TH + (nk + 1) * 512],
                                in0=sact[:, nk * 512:(nk + 1) * 512],
                                in1=ps_gu[2 + nk][:], op=ALU.mult))
                        fv_act[(gstep, nk)] = fv.v
                fm_gu_prev = fm.v
                FV_ACT = fv.v

                for ht in range(HC):
                    gd, wtd = emit_wload(dsrc[ht], 2)
                    bd, (ctsem, ctv) = emit_wcast(gd, wtd, 2, "g")
                    @block.tensor
                    def _(t, ht=ht, bd=bd, ctsem=ctsem, ctv=ctv, at=FV_ACT, yp=fv_yev_prev):
                        t.wait_ge(ctsem, ctv)
                        t.wait_ge(S_FFV, at)
                        if yp:
                            t.wait_ge(S_FFV, yp)
                        last = None
                        for ic in range(IT):
                            for nk in range(2):
                                last = t.matmul(out=ps_y[:, nk * 512:(nk + 1) * 512],
                                                lhsT=wbf[:, bd * 1024 + ic * P:bd * 1024 + (ic + 1) * P],
                                                rhs=act_bf[:, ic * TH + nk * 512:ic * TH + (nk + 1) * 512],
                                                start=(ic == 0), stop=(ic == IT - 1))
                        fm.inc(last)
                    wbf_free[bd] = fm.v
                    wbf_free[bd + 1] = fm.v
                    @block.vector
                    def _(v, ht=ht, mt=fm.v, ei=ei):
                        v.wait_ge(S_FFM, mt)
                        if ei == 0:
                            fv.inc(v.tensor_copy(out=y_fm[:, ht * TH:(ht + 1) * TH],
                                                 in_=ps_y[:]))
                        else:
                            fv.inc(v.tensor_tensor(out=y_fm[:, ht * TH:(ht + 1) * TH],
                                                   in0=y_fm[:, ht * TH:(ht + 1) * TH],
                                                   in1=ps_y[:], op=ALU.add))
                    fv_yev_prev = fv.v
            FV_FFN = fv.v
            if _DEBUG_TAP == "yffn":
                @block.sync
                def _(sync, hf=hf, vt=fv.v):
                    sync.wait_ge(S_FFV, vt)
                    st.inc(sync.dma_start(
                        out=dbg_h.rearrange("(a p) f -> p a f", p=P)[:, hf * 8:(hf + 1) * 8, :],
                        in_=y_fm.rearrange("p (a f) -> p a f", a=8)))

            v_tp3 = {}
            n_tp3 = 0
            for c in range(HC):
                for tt in range(8):
                    @block.tensor
                    def _(t, c=c, tt=tt, n_tp3=n_tp3, fsv=fs.v):
                        if n_tp3 == 0:
                            t.wait_ge(S_FFV, FV_FFN)
                            t.wait_ge(S_FFS, fsv)
                        if n_tp3 >= 2:
                            t.wait_ge(S_FFV, v_tp3[n_tp3 - 2])
                        fm.inc(t.transpose(out=pst_of(n_tp3),
                                           in_=y_fm[:, c * TH + tt * P:c * TH + (tt + 1) * P],
                                           identity=ident[:]))
                    @block.vector
                    def _(v, c=c, tt=tt, n_tp3=n_tp3, mt=fm.v):
                        v.wait_ge(S_FFM, mt)
                        fv.inc(v.tensor_tensor(
                            out=r_fm[:, tt * H + c * P:tt * H + (c + 1) * P],
                            in0=x1_tm[:, tt * H + c * P:tt * H + (c + 1) * P],
                            in1=pst_of(n_tp3), op=ALU.add))
                    v_tp3[n_tp3] = fv.v
                    n_tp3 += 1
            FV_R2 = fv.v

            for tt in range(8):
                emit_ln(r_fm, tt, FV_R2 if tt == 0 else None)
                @block.sync
                def _(sync, tt=tt, hf=hf, vt=fv.v):
                    sync.wait_ge(S_FFV, vt)
                    st.inc(sync.dma_start(
                        out=out_h[hofs + tt * P:hofs + (tt + 1) * P, :],
                        in_=r_fm[:, tt * H:(tt + 1) * H]))
            ST_PREV_HALF = st.v

        stackD.close()

        @block.sync
        def _(sync):
            sync.wait_ge(S_ST, st.v)

    return nc


def _prep_inputs(ins):
    x = np.ascontiguousarray(np.asarray(ins["L0_attn_x"])[0])
    xT = np.ascontiguousarray(x.T)
    w1 = np.ascontiguousarray(
        np.concatenate([np.asarray(ins["L0_attn_x_proj_w"]).T,
                        np.asarray(ins["L0_attn_x_proj_b_w"]).T], axis=1))
    dtw_f = np.asarray(ins["L0_attn_dt_proj_w"])
    dtw_b = np.asarray(ins["L0_attn_dt_proj_b_w"])
    alog_f = np.asarray(ins["L0_attn_A_log"])
    alog_b = np.asarray(ins["L0_attn_A_b_log"])
    dtb_f = np.asarray(ins["L0_attn_dt_proj_b"])
    dtb_b = np.asarray(ins["L0_attn_dt_proj_b_b"])
    dv_f = np.asarray(ins["L0_attn_D"])
    dv_b = np.asarray(ins["L0_attn_D_b"])

    gw = np.ascontiguousarray(np.asarray(ins["L0_ffn_gate_w"]))
    shg = np.ascontiguousarray(np.asarray(ins["L0_ffn_sh_gate"])
                               .reshape(HC, P, IT, P).transpose(2, 1, 0, 3))
    shu = np.ascontiguousarray(np.asarray(ins["L0_ffn_sh_up"])
                               .reshape(HC, P, IT, P).transpose(2, 1, 0, 3))
    shd = np.ascontiguousarray(np.asarray(ins["L0_ffn_sh_down"])
                               .reshape(IT, P, HC, P).transpose(2, 1, 0, 3))
    rtg = np.ascontiguousarray(np.asarray(ins["L0_ffn_rt_gate"])
                               .reshape(E, HC, P, IT, P).transpose(0, 3, 2, 1, 4))
    rtu = np.ascontiguousarray(np.asarray(ins["L0_ffn_rt_up"])
                               .reshape(E, HC, P, IT, P).transpose(0, 3, 2, 1, 4))
    rtd = np.ascontiguousarray(np.asarray(ins["L0_ffn_rt_down"])
                               .reshape(E, IT, P, HC, P).transpose(0, 3, 2, 1, 4))

    in_maps = []
    for k in range(NCORES):
        sl = slice(P * k, P * (k + 1))
        tok = slice(TC * k, TC * (k + 1))
        in_maps.append({
            "xk_tok": np.ascontiguousarray(xT[:, tok]),
            "xk_str": np.ascontiguousarray(xT[sl, :]),
            "w1": w1,
            "dtw": np.ascontiguousarray(
                np.concatenate([dtw_f[sl].T, dtw_b[sl].T], axis=1)),
            "dtb": np.ascontiguousarray(np.stack([dtb_f[sl], dtb_b[sl]], axis=1)),
            "alog": np.ascontiguousarray(np.concatenate([alog_f[sl], alog_b[sl]], axis=1)),
            "dvec": np.ascontiguousarray(np.stack([dv_f[sl], dv_b[sl]], axis=1)),
            "gw": gw, "shg": shg, "shu": shu, "shd": shd,
            "rtg": rtg, "rtu": rtu, "rtd": rtd,
        })
    return in_maps


def kernel(**inputs):
    in_maps = _prep_inputs(inputs)
    nc = build_program(stage=2)
    res = run_bass_kernel_spmd(nc, in_maps, core_ids=list(range(NCORES)))
    out = np.concatenate([res.results[k]["out"] for k in range(NCORES)], axis=0)
    return out.reshape(1, T, H)


# version 12
# speedup vs baseline: 1.1762x; 1.1762x over previous
import numpy as np
from contextlib import ExitStack
import concourse.bass as bass
import concourse.mybir as mybir
from concourse.bass_utils import run_bass_kernel_spmd
from concourse.replica_groups import maybe_share_collective_output_space

F32 = mybir.dt.float32
F32R = mybir.dt.float32r
BF16 = mybir.dt.bfloat16
AF = mybir.ActivationFunctionType
ALU = mybir.AluOpType
AX = mybir.AxisListType

P = 128
T = 16384
TC = 2048
TH = 1024
H = 1024
DSTATE = 16
DTR = 32
E = 8
I = 2048
TB = 1024
NB = T // TB
NCORES = 8
HC = H // P
IT = I // P
RG = [list(range(NCORES))]


_DEBUG_DIRS = (0, 1)
_DEBUG_DUMP = "r"
_DEBUG_TAP = None


class Ctr:
    def __init__(self, sem, step=1):
        self.sem, self.v, self.step = sem, 0, step

    def inc(self, inst, n=None):
        n = self.step if n is None else n
        inst.then_inc(self.sem, n)
        self.v += n
        return self.v


def build_program(stage=2):
    nc = bass.Bass("TRN2", target_bir_lowering=False, debug=False)

    xk_tok_h = nc.dram_tensor("xk_tok", [H, TC], F32, kind="ExternalInput").ap()
    xk_str_h = nc.dram_tensor("xk_str", [P, T], F32, kind="ExternalInput").ap()
    w1_h = nc.dram_tensor("w1", [H, P], F32, kind="ExternalInput").ap()
    dtw_h = nc.dram_tensor("dtw", [DTR, 2 * P], F32, kind="ExternalInput").ap()
    dtb_h = nc.dram_tensor("dtb", [P, 2], F32, kind="ExternalInput").ap()
    alog_h = nc.dram_tensor("alog", [P, 2 * DSTATE], F32, kind="ExternalInput").ap()
    dvec_h = nc.dram_tensor("dvec", [P, 2], F32, kind="ExternalInput").ap()
    gw_h = nc.dram_tensor("gw", [H, E], F32, kind="ExternalInput").ap()
    shg_h = nc.dram_tensor("shg", [IT, P, HC, P], F32, kind="ExternalInput").ap()
    shu_h = nc.dram_tensor("shu", [IT, P, HC, P], F32, kind="ExternalInput").ap()
    shd_h = nc.dram_tensor("shd", [HC, P, IT, P], F32, kind="ExternalInput").ap()
    rtg_h = nc.dram_tensor("rtg", [E, IT, P, HC, P], F32, kind="ExternalInput").ap()
    rtu_h = nc.dram_tensor("rtu", [E, IT, P, HC, P], F32, kind="ExternalInput").ap()
    rtd_h = nc.dram_tensor("rtd", [E, HC, P, IT, P], F32, kind="ExternalInput").ap()

    if stage >= 2:
        out_h = nc.dram_tensor("out", [TC, H], F32, kind="ExternalOutput").ap()
        dbg_h = (nc.dram_tensor("dbg", [TC, H], F32, kind="ExternalOutput").ap()
                 if _DEBUG_TAP else None)
    if stage == 1:
        rdump_h = nc.dram_tensor("rdump", [P, T], F32, kind="ExternalOutput").ap()

    dbc_loc = nc.dram_tensor("dbc_loc", [P, TC], F32, kind="Internal").ap()
    ag_space = maybe_share_collective_output_space("AllGather", RG)
    dbc_ag = nc.dram_tensor("dbc_ag", [NCORES * P, TC], F32, kind="Internal",
                            addr_space=ag_space).ap()
    r_loc = nc.dram_tensor("r_loc", [P, T], F32, kind="Internal").ap()
    rag_out = nc.dram_tensor("rag_out", [NCORES * P, T], F32, kind="Internal",
                             addr_space=ag_space).ap()

    ctx = ExitStack()
    with ctx:
        x_str = ctx.enter_context(nc.sbuf_tensor("x_str", [P, T], F32))
        y_sb = ctx.enter_context(nc.sbuf_tensor("y_sb", [P, T], F32))
        chain = ctx.enter_context(nc.sbuf_tensor("chain", [P, 2 * DSTATE], F32))
        ident = ctx.enter_context(nc.sbuf_tensor("ident", [P, P], F32))
        selb16 = ctx.enter_context(nc.sbuf_tensor("selb16", [80, DSTATE * P], F32R))
        selb8 = ctx.enter_context(nc.sbuf_tensor("selb8", [E, E * P], F32R))
        dtwr = ctx.enter_context(nc.sbuf_tensor("dtwr", [DTR, 2 * P], F32R))
        dtb_sb = ctx.enter_context(nc.sbuf_tensor("dtb_sb", [P, 2], F32))
        acol = ctx.enter_context(nc.sbuf_tensor("acol", [P, 2 * DSTATE], F32))
        dsum = ctx.enter_context(nc.sbuf_tensor("dsum", [P, 2], F32))
        gw_bf = ctx.enter_context(nc.sbuf_tensor("gw_bf", [P, HC * E], BF16))

        sems = {}
        for name in ["s_ld", "s_dbcd", "s_mm", "s_sc", "s_v", "s_g", "s_coll",
                     "s_st", "s_wd", "s_ffv", "s_ffs", "s_ffm",
                     "s_xc0", "s_xc1", "s_xstr", "s_rl", "s_r0", "s_r1",
                     "s_w0", "s_w1", "s_w2", "s_w3", "s_w4", "s_w5"]:
            sems[name] = ctx.enter_context(nc.semaphore(name))
        S_LD, S_DBCD, S_MM, S_SC, S_V, S_G = (sems[k] for k in
                                              ["s_ld", "s_dbcd", "s_mm", "s_sc", "s_v", "s_g"])
        S_COLL, S_ST, S_WD, S_FFV, S_FFS, S_FFM = (sems[k] for k in
                                                   ["s_coll", "s_st", "s_wd", "s_ffv", "s_ffs", "s_ffm"])
        block = ctx.enter_context(nc.Block())

        ld = Ctr(S_LD, 16)
        dbcd = Ctr(S_DBCD, 16)
        mm = Ctr(S_MM)
        sc = Ctr(S_SC)
        vv = Ctr(S_V)
        gg = Ctr(S_G)
        coll = Ctr(S_COLL)
        st = Ctr(S_ST, 16)

        stackA = ExitStack()
        xc_sb = stackA.enter_context(nc.sbuf_tensor("xc_sb", [P, 2 * TC], F32))
        xc_r = stackA.enter_context(nc.sbuf_tensor("xc_r", [P, 2 * TC], F32R))
        w1r = stackA.enter_context(nc.sbuf_tensor("w1r", [P, HC * P], F32R))
        w1f = stackA.enter_context(nc.sbuf_tensor("w1f", [P, HC * P], F32))
        dtwf = stackA.enter_context(nc.sbuf_tensor("dtwf", [DTR, 2 * P], F32))
        sel_f = stackA.enter_context(nc.sbuf_tensor("sel_f", [80, DSTATE * P], F32))
        ps_dbc = stackA.enter_context(nc.psum_tensor("ps_dbc", [P, TC], F32))

        @block.sync
        def _(sync):
            sync.dma_start(out=x_str[:], in_=xk_str_h[:]).then_inc(sems["s_xstr"], 16)
            ld.inc(sync.dma_start(out=w1f.ap().rearrange("p (c m) -> p c m", c=HC),
                                  in_=w1_h.rearrange("(c p) m -> p c m", p=P)))
            ld.inc(sync.dma_start(out=dtwf[:], in_=dtw_h[:]))
            ld.inc(sync.dma_start(out=dtb_sb[:], in_=dtb_h[:]))
            ld.inc(sync.dma_start(out=acol[:], in_=alog_h[:]))
            ld.inc(sync.dma_start(out=dsum[:], in_=dvec_h[:]))
            sync.wait_ge(S_LD, 16 * 5)
            for c in range(HC):
                if c >= 2:
                    sync.wait_ge(S_MM, c - 1)
                sync.dma_start(
                    out=xc_sb[:, (c % 2) * TC:(c % 2 + 1) * TC],
                    in_=xk_tok_h[c * P:(c + 1) * P, :]).then_inc(
                        sems["s_xc0" if c % 2 == 0 else "s_xc1"], 16)
        LD_ALL = ld.v

        @block.gpsimd
        def _(g):
            g.memset(ident[:], 0.0)
            g.affine_select(out=ident[:], in_=ident[:], compare_op=ALU.not_equal,
                            fill=1.0, base=0, pattern=[[-1, P]], channel_multiplier=1)
            g.memset(sel_f[:], 0.0)
            for rb in (32, 64):
                g.affine_select(out=sel_f[rb:rb + DSTATE, :], in_=sel_f[rb:rb + DSTATE, :],
                                compare_op=ALU.not_equal, fill=1.0, base=0,
                                pattern=[[1, DSTATE], [0, P]], channel_multiplier=-1)
                g.tensor_copy(out=selb16[rb:rb + DSTATE, :], in_=sel_f[rb:rb + DSTATE, :])
            g.memset(sel_f[0:E, 0:E * P], 0.0)
            g.affine_select(out=sel_f[0:E, 0:E * P], in_=sel_f[0:E, 0:E * P],
                            compare_op=ALU.not_equal, fill=1.0, base=0,
                            pattern=[[1, E], [0, P]], channel_multiplier=-1)
            g.tensor_copy(out=selb8[:], in_=sel_f[0:E, 0:E * P])
            gg.inc(g.memset(y_sb[:], 0.0))
        GG_CONST = gg.v

        @block.scalar
        def _(s):
            s.wait_ge(S_LD, 16 * 5)
            sc.inc(s.activation(out=acol[:], in_=acol[:], func=AF.Exp))
            s.wait_ge(S_SC, sc.v)
            sc.inc(s.mul(out=acol[:], in_=acol[:], mul=-1.0))
        SC_ACOL = sc.v

        @block.vector
        def _(v):
            v.wait_ge(S_LD, 16 * 5)
            vv.inc(v.tensor_tensor(out=dsum[:, 0:1], in0=dsum[:, 0:1], in1=dsum[:, 1:2], op=ALU.add))
            v.wait_ge(S_V, vv.v)
            vv.inc(v.tensor_scalar_add(out=dsum[:, 0:1], in0=dsum[:, 0:1], scalar1=1.0))
            v.wait_ge(S_V, vv.v)
            vv.inc(v.tensor_copy(out=w1r[:], in_=w1f[:]))
            vv.inc(v.tensor_copy(out=dtwr[:], in_=dtwf[:]))
        VV_SMALL = vv.v

        for c in range(HC):
            @block.vector
            def _(v, c=c):
                v.wait_ge(sems["s_xc0" if c % 2 == 0 else "s_xc1"], 16 * (c // 2 + 1))
                vv.inc(v.tensor_copy(out=xc_r[:, (c % 2) * TC:(c % 2 + 1) * TC],
                                     in_=xc_sb[:, (c % 2) * TC:(c % 2 + 1) * TC]))

            @block.tensor
            def _(t, c=c, vt=vv.v):
                t.wait_ge(S_V, vt)
                last = None
                for nk in range(TC // 512):
                    last = t.matmul(
                        out=ps_dbc[:, nk * 512:(nk + 1) * 512],
                        lhsT=w1r[:, c * P:(c + 1) * P],
                        rhs=xc_r[:, (c % 2) * TC + nk * 512:(c % 2) * TC + (nk + 1) * 512],
                        start=(c == 0), stop=(c == HC - 1))
                mm.inc(last)
        MM_DBC = mm.v

        @block.vector
        def _(v):
            v.wait_ge(S_MM, MM_DBC)
            vv.inc(v.tensor_copy(out=xc_sb[:, 0:TC], in_=ps_dbc[:]))
        VV_DBCE = vv.v

        @block.sync
        def _(sync):
            sync.wait_ge(S_V, VV_DBCE)
            st.inc(sync.dma_start(out=dbc_loc[:], in_=xc_sb[:, 0:TC]))
        ST_DBC = st.v

        @block.gpsimd
        def _(g):
            g.wait_ge(S_ST, ST_DBC)
            coll.inc(g.collective_compute("AllGather", ALU.bypass, replica_groups=RG,
                                          ins=[dbc_loc], outs=[dbc_ag]))
        COLL_AG = coll.v

        stackA.close()

        stackB = ExitStack()
        dbc_sb = stackB.enter_context(nc.sbuf_tensor("dbc_sb", [96, TB], F32))
        dbc_r = stackB.enter_context(nc.sbuf_tensor("dbc_r", [96, 2 * TB], F32R))
        spu = stackB.enter_context(nc.sbuf_tensor("spu", [P, TB], F32))
        delta = stackB.enter_context(nc.sbuf_tensor("delta", [P, TB], F32))
        a_sb = stackB.enter_context(nc.sbuf_tensor("a_sb", [P, 2 * TB], F32))
        bx_sb = stackB.enter_context(nc.sbuf_tensor("bx_sb", [P, TB], F32))
        h_sb = stackB.enter_context(nc.sbuf_tensor("h_sb", [P, 2 * TB], F32))
        hc_sb = stackB.enter_context(nc.sbuf_tensor("hc_sb", [P, 2 * TB], F32))
        bbc_sb = stackB.enter_context(nc.sbuf_tensor("bbc_sb", [P, 2 * TB], F32))
        cbc_sb = stackB.enter_context(nc.sbuf_tensor("cbc_sb", [P, 2 * TB], F32))
        pool = [stackB.enter_context(nc.psum_tensor(f"pp{i}", [P, TB], F32))
                for i in range(4)]
        pool_guard = [(None, 0)] * 4
        pool_i = [0]

        def next_slot():
            s_ = pool_i[0] % 4
            pool_i[0] += 1
            return s_

        dbc_ag_v = dbc_ag.rearrange("(r p) t -> r p t", p=P)
        blk_order = []
        if 0 in _DEBUG_DIRS:
            blk_order += [(0, bi) for bi in range(NB)]
        if 1 in _DEBUG_DIRS:
            blk_order += [(1, bi) for bi in reversed(range(NB))]

        v_round = {}
        v_bx = {}
        v_scan = {}
        v_yadd = {}
        g_iter = {}
        sc_iter = {}
        mm_end = {}
        sc_sp_prev = None
        v_u_prev = None
        NS = DSTATE

        def gs_of(idx, s):
            return idx * NS + s

        def by_gs(gs):
            return (gs // NS, gs % NS)

        for idx, (dr, bi) in enumerate(blk_order):
            pb = idx % 2
            rowbase = 64 * dr
            sender = bi // 2
            col0 = (bi % 2) * TB

            @block.sync
            def _(sync, pb=pb, rowbase=rowbase, sender=sender, col0=col0, idx=idx):
                if idx == 0:
                    sync.wait_ge(S_COLL, COLL_AG)
                if idx >= 1:
                    sync.wait_ge(S_V, v_round[idx - 1])
                dbcd.inc(sync.dma_start(
                    out=dbc_sb[0:DTR, :],
                    in_=dbc_ag_v[sender, rowbase:rowbase + DTR, col0:col0 + TB]))
                dbcd.inc(sync.dma_start(
                    out=dbc_sb[32:32 + DSTATE, :],
                    in_=dbc_ag_v[sender, rowbase + 32:rowbase + 48, col0:col0 + TB]))
                dbcd.inc(sync.dma_start(
                    out=dbc_sb[64:64 + DSTATE, :],
                    in_=dbc_ag_v[sender, rowbase + 48:rowbase + 64, col0:col0 + TB]))
            DBCD_T = dbcd.v

            @block.vector
            def _(v, pb=pb, idx=idx):
                v.wait_ge(S_DBCD, DBCD_T)
                if idx >= 2:
                    v.wait_ge(S_MM, mm_end[idx - 2])
                o = pb * TB
                v.tensor_copy(out=dbc_r[0:DTR, o:o + TB], in_=dbc_sb[0:DTR, :])
                v.tensor_copy(out=dbc_r[32:48, o:o + TB], in_=dbc_sb[32:48, :])
                vv.inc(v.tensor_copy(out=dbc_r[64:80, o:o + TB], in_=dbc_sb[64:80, :]))
            v_round[idx] = vv.v

            dslot = next_slot()
            @block.tensor
            def _(t, dr=dr, pb=pb, vt=vv.v, idx=idx, dslot=dslot, gd=pool_guard[dslot]):
                t.wait_ge(S_V, vt)
                if idx == 0:
                    t.wait_ge(S_SC, SC_ACOL)
                if gd[1]:
                    t.wait_ge(gd[0], gd[1])
                last = None
                for nk in range(TB // 512):
                    last = t.matmul(out=pool[dslot][:, nk * 512:(nk + 1) * 512],
                                    lhsT=dtwr[:, dr * P:(dr + 1) * P],
                                    rhs=dbc_r[0:DTR, pb * TB + nk * 512:pb * TB + (nk + 1) * 512],
                                    start=True, stop=True)
                mm.inc(last)
            MM_D = mm.v

            @block.scalar
            def _(s, dr=dr, idx=idx, dslot=dslot):
                s.wait_ge(S_MM, MM_D)
                if idx >= 1:
                    s.wait_ge(S_V, v_bx[gs_of(idx - 1, NS - 1)])
                s.activation(out=spu[:], in_=pool[dslot][:], func=AF.Exp,
                             bias=dtb_sb[:, dr:dr + 1])
                sc.inc(s.activation(out=delta[:], in_=spu[:], func=AF.Ln, bias=1.0))
            sc_sp_prev = sc.v
            pool_guard[dslot] = (S_SC, sc.v)

            @block.vector
            def _(v, bi=bi, idx=idx, sct=sc.v):
                v.wait_ge(S_SC, sct)
                if idx == 0:
                    v.wait_ge(sems["s_xstr"], 16)
                vv.inc(v.tensor_tensor(out=spu[:], in0=delta[:],
                                       in1=x_str[:, bi * TB:(bi + 1) * TB], op=ALU.mult))
            v_u_prev = vv.v

            for s in range(NS):
                sb = s % 2
                gs = gs_of(idx, s)
                sb2 = gs % 2

                bslot = next_slot()
                cslot = next_slot()
                @block.tensor
                def _(t, s=s, pb=pb, bslot=bslot, cslot=cslot,
                      gb=pool_guard[bslot], gc=pool_guard[cslot], vt=v_round[idx]):
                    t.wait_ge(S_V, vt)
                    if gb[1]:
                        t.wait_ge(gb[0], gb[1])
                    if gc[1]:
                        t.wait_ge(gc[0], gc[1])
                    last = None
                    for nk in range(TB // 512):
                        t.matmul(out=pool[bslot][:, nk * 512:(nk + 1) * 512],
                                 lhsT=selb16[32:48, s * P:(s + 1) * P],
                                 rhs=dbc_r[32:48, pb * TB + nk * 512:pb * TB + (nk + 1) * 512],
                                 start=True, stop=True)
                        last = t.matmul(out=pool[cslot][:, nk * 512:(nk + 1) * 512],
                                        lhsT=selb16[64:80, s * P:(s + 1) * P],
                                        rhs=dbc_r[64:80, pb * TB + nk * 512:pb * TB + (nk + 1) * 512],
                                        start=True, stop=True)
                    mm.inc(last)
                MM_BC = mm.v

                @block.scalar
                def _(s_, s=s, sb=sb, dr=dr, gs=gs, cslot=cslot):
                    s_.wait_ge(S_MM, MM_BC)
                    if gs >= 2:
                        pgs = gs - 2
                        s_.wait_ge(S_G, g_iter[pgs])
                        s_.wait_ge(S_V, v_scan[pgs])
                    s_.copy(out=cbc_sb[:, sb * TB:(sb + 1) * TB], in_=pool[cslot][:])
                    sc.inc(s_.activation(out=a_sb[:, sb * TB:(sb + 1) * TB],
                                         in_=delta[:], func=AF.Exp,
                                         scale=acol[:, dr * DSTATE + s:dr * DSTATE + s + 1]))
                sc_iter[gs] = sc.v
                pool_guard[cslot] = (S_SC, sc.v)

                @block.vector
                def _(v, s=s, sb=sb, sb2=sb2, dr=dr, bi=bi, idx=idx, gs=gs,
                      bslot=bslot, mt=MM_BC, sct=sc.v):
                    v.wait_ge(S_MM, mt)
                    v.wait_ge(S_SC, sct)
                    vv.inc(v.tensor_tensor(out=bx_sb[:], in0=spu[:],
                                           in1=pool[bslot][:], op=ALU.mult))
                    v_bx[gs] = vv.v
                    v.wait_ge(S_V, vv.v)
                    if gs >= 2:
                        v.wait_ge(S_G, g_iter[gs - 2])
                    first = (idx == 0) or (dr == 1 and blk_order[idx - 1][0] == 0)
                    ci = dr * DSTATE + s
                    init = 0.0 if first else chain[:, ci:ci + 1]
                    ho = sb2 * TB
                    if dr == 0:
                        ii = v.tensor_tensor_scan(out=h_sb[:, ho:ho + TB],
                                                  data0=a_sb[:, sb * TB:(sb + 1) * TB],
                                                  data1=bx_sb[:], initial=init,
                                                  op0=ALU.mult, op1=ALU.add)
                    else:
                        ii = v.tensor_tensor_scan(out=h_sb[:, ho:ho + TB],
                                                  data0=a_sb[:, sb * TB:(sb + 1) * TB][:, ::-1],
                                                  data1=bx_sb[:, ::-1], initial=init,
                                                  op0=ALU.mult, op1=ALU.add)
                    vv.inc(ii)
                    v_scan[gs] = vv.v
                pool_guard[bslot] = (S_V, v_bx[gs])

                @block.gpsimd
                def _(g, sb=sb, sb2=sb2, dr=dr, bi=bi, s=s, gs=gs, vt=v_scan[gs], sct=sc.v):
                    g.wait_ge(S_V, vt)
                    g.wait_ge(S_SC, sct)
                    ho = sb2 * TB
                    if dr == 0:
                        gg.inc(g.tensor_tensor(out=hc_sb[:, ho:ho + TB],
                                               in0=h_sb[:, ho:ho + TB],
                                               in1=cbc_sb[:, sb * TB:(sb + 1) * TB],
                                               op=ALU.mult))
                    else:
                        gg.inc(g.tensor_tensor(out=hc_sb[:, ho:ho + TB],
                                               in0=h_sb[:, ho:ho + TB],
                                               in1=cbc_sb[:, sb * TB:(sb + 1) * TB][:, ::-1],
                                               op=ALU.mult))
                    g.wait_ge(S_G, gg.v)
                    ci = dr * DSTATE + s
                    g.tensor_copy(out=chain[:, ci:ci + 1],
                                  in_=h_sb[:, ho + TB - 1:ho + TB])
                    src = hc_sb[:, ho:ho + TB]
                    if dr == 1:
                        src = src[:, ::-1]
                    gg.inc(g.tensor_tensor(out=y_sb[:, bi * TB:(bi + 1) * TB],
                                           in0=y_sb[:, bi * TB:(bi + 1) * TB],
                                           in1=src, op=ALU.add))
                g_iter[gs] = gg.v
                v_yadd[gs] = gg.v

            mm_end[idx] = mm.v

        r_st = []
        for j in range(NCORES):
            rbuf = a_sb if j % 2 == 0 else cbc_sb

            @block.vector
            def _(v, j=j, rbuf=rbuf):
                if j == 0:
                    lastgs = gs_of(len(blk_order) - 1, NS - 1)
                    v.wait_ge(S_G, g_iter[lastgs])
                if j >= 2:
                    v.wait_ge(sems["s_r0" if j % 2 == 0 else "s_r1"], 16 * (j // 2))
                ii = None
                for q in range(2):
                    blk = slice(j * TC + q * TB, j * TC + (q + 1) * TB)
                    ii = v.scalar_tensor_tensor(out=rbuf[:, q * TB:(q + 1) * TB],
                                                in0=x_str[:, blk], scalar=dsum[:, 0:1],
                                                in1=y_sb[:, blk], op0=ALU.mult, op1=ALU.add)
                vv.inc(ii)

            @block.sync
            def _(sync, j=j, rbuf=rbuf, vt=vv.v):
                sync.wait_ge(S_V, vt)
                if stage == 1:
                    src = rbuf[:] if _DEBUG_DUMP == "r" else y_sb[:, j * TC:(j + 1) * TC]
                    st.inc(sync.dma_start(out=rdump_h[:, j * TC:(j + 1) * TC], in_=src))
                sync.dma_start(out=r_loc[:, j * TC:(j + 1) * TC],
                               in_=rbuf[:]).then_inc(
                                   sems["s_r0" if j % 2 == 0 else "s_r1"], 16)
            r_st.append(st.v)

        @block.gpsimd
        def _(g, stv=st.v):
            g.wait_ge(S_ST, stv)
            g.wait_ge(sems["s_r0"], 16 * 4)
            g.wait_ge(sems["s_r1"], 16 * 4)
            coll.inc(g.collective_compute("AllGather", ALU.bypass, replica_groups=RG,
                                          ins=[r_loc], outs=[rag_out]))
        COLL_A2A = coll.v

        stackB.close()

        if stage == 1:
            @block.sync
            def _(sync):
                sync.wait_ge(S_COLL, COLL_A2A)
                sync.wait_ge(S_ST, st.v)
            return nc

        stackD = ExitStack()
        wld = stackD.enter_context(nc.sbuf_tensor("wld", [P, 6 * 1024], F32))
        wbf = stackD.enter_context(nc.sbuf_tensor("wbf", [P, 6 * 1024], BF16))
        sact = stackD.enter_context(nc.sbuf_tensor("sact", [P, TH], F32))
        stat = stackD.enter_context(nc.sbuf_tensor("stat", [P, 16], F32))
        lgsm = stackD.enter_context(nc.sbuf_tensor("lgsm", [E, TH], F32))
        sm_fm_t = stackD.enter_context(nc.sbuf_tensor("sm_fm_t", [E, TH], F32R))
        lgtm = stackD.enter_context(nc.sbuf_tensor("lgtm", [P, 8 * E], F32))
        smtm = stackD.enter_context(nc.sbuf_tensor("smtm", [P, 8 * E], F32))
        sm_bf = stackD.enter_context(nc.sbuf_tensor("sm_bf", [P, TH], BF16))
        selb8d = stackD.enter_context(nc.sbuf_tensor("selb8d", [E, E * P], F32R))
        ps_gu = [stackD.enter_context(nc.psum_tensor(f"ps_gu{i}", [P, 512], F32))
                 for i in range(4)]
        ps_y = stackD.enter_context(nc.psum_tensor("ps_y", [P, TH], F32))
        ps_ty = stackD.enter_context(nc.psum_tensor("ps_ty", [P, TH], F32))

        r_fm = x_str[:, 0:8 * TH]
        x1_tm = x_str[:, 8 * TH:16 * TH]
        x1_bf = x_str[:, 0:4 * TH].bitcast(BF16)
        xs_bf = x_str[:, 4 * TH:8 * TH].bitcast(BF16)
        y_fm = y_sb[:, 0:8 * TH]
        act_bf = y_sb[:, 8 * TH:16 * TH].bitcast(BF16)
        sm_fm = sm_fm_t.ap()
        lg_sb = lgsm.ap()

        fm = Ctr(S_FFM)
        fv = Ctr(S_FFV)
        fs = Ctr(S_FFS)
        wd = Ctr(S_WD, 16)

        def pst_of(j):
            return ps_ty[:, (j % 2) * 512:(j % 2) * 512 + P]

        @block.gpsimd
        def _(g):
            g.wait_ge(S_COLL, COLL_A2A)
            g.memset(sact[0:E, 0:E * P], 0.0)
            g.affine_select(out=sact[0:E, 0:E * P], in_=sact[0:E, 0:E * P],
                            compare_op=ALU.not_equal, fill=1.0, base=0,
                            pattern=[[1, E], [0, P]], channel_multiplier=-1)
            gg.inc(g.tensor_copy(out=selb8d[:], in_=sact[0:E, 0:E * P]))
        GG_SELB8 = gg.v

        @block.sync
        def _(sync):
            sync.wait_ge(S_COLL, COLL_A2A)
            wd.inc(sync.dma_start(out=wld[:, 0:HC * E].rearrange("p (c m) -> p c m", c=HC),
                                  in_=gw_h.rearrange("(c p) m -> p c m", p=P)))
        @block.vector
        def _(v, wt=wd.v):
            v.wait_ge(S_WD, wt)
            fv.inc(v.tensor_copy(out=gw_bf[:], in_=wld[:, 0:HC * E]))
        FV_GW = fv.v

        rag_v = rag_out.rearrange("(c p) t -> p c t", p=P)

        NGRAN = 6
        wsems = [sems[f"s_w{i}"] for i in range(NGRAN)]
        wld_free = [0] * NGRAN
        wld_free_sem = [None] * NGRAN
        wld_free_sem[0] = S_FFV
        wld_free[0] = FV_GW
        wld_dma = [(None, 0)] * NGRAN
        wld_uses = [0] * NGRAN
        wbf_free = [0] * NGRAN
        ring = {"w": 0, "b": 0}

        def emit_wload(src_ap, ngran):
            while True:
                cand = ring["w"] % NGRAN
                if ngran == 2 and cand % 2 == 1:
                    ring["w"] += 1
                    continue
                g0 = cand
                ring["w"] += ngran
                break
            waits = [(wld_free_sem[g0 + k], wld_free[g0 + k]) for k in range(ngran)]
            wld_uses[g0] += 1
            @block.sync
            def _(sync, g0=g0, src_ap=src_ap, ngran=ngran, waits=waits):
                for sem_, wt in waits:
                    if wt:
                        sync.wait_ge(sem_, wt)
                sync.dma_start(
                    out=wld[:, g0 * 1024:(g0 + ngran) * 1024].rearrange(
                        "p (c m) -> p c m", m=P),
                    in_=src_ap).then_inc(wsems[g0], 16)
            mark = (wsems[g0], 16 * wld_uses[g0])
            for k in range(ngran):
                wld_dma[g0 + k] = mark
            return g0, mark

        def emit_wcast(g0, wt, ngran, eng):
            while True:
                cand = ring["b"] % NGRAN
                if ngran == 2 and cand % 2 == 1:
                    ring["b"] += 1
                    continue
                b0 = cand
                ring["b"] += ngran
                break
            waits = [wbf_free[b0 + k] for k in range(ngran)]

            def body(e_):
                e_.wait_ge(wt[0], wt[1])
                for w_ in waits:
                    if w_:
                        e_.wait_ge(S_FFM, w_)

            if eng == "s":
                @block.scalar
                def _(s):
                    body(s)
                    fs.inc(s.copy(out=wbf[:, b0 * 1024:(b0 + ngran) * 1024],
                                  in_=wld[:, g0 * 1024:(g0 + ngran) * 1024]))
                mark_sem, mark = S_FFS, fs.v
            else:
                @block.gpsimd
                def _(g):
                    body(g)
                    gg.inc(g.tensor_copy(out=wbf[:, b0 * 1024:(b0 + ngran) * 1024],
                                         in_=wld[:, g0 * 1024:(g0 + ngran) * 1024]))
                mark_sem, mark = S_G, gg.v
            for k in range(ngran):
                wld_free_sem[g0 + k] = mark_sem
                wld_free[g0 + k] = mark
            return b0, (mark_sem, mark)

        ST_PREV_HALF = 0
        fv_act = {}
        fs_silu = {}
        gu_step = [0]
        fv_yev_prev = 0
        fm_gu_prev = 0
        fs_sm_prev = 0

        for hf in range(2):
            hofs = hf * TH

            @block.sync
            def _(sync, hf=hf, stp=ST_PREV_HALF, fmv=fm.v):
                if hf == 0:
                    sync.wait_ge(S_COLL, COLL_A2A)
                else:
                    sync.wait_ge(S_ST, stp)
                    sync.wait_ge(S_FFM, fmv)
                pid = nc.partition_id([mybir.EngineType.SP])
                sync.dma_start(out=r_fm.rearrange("p (c t) -> p c t", c=HC),
                               in_=rag_v[:, :, bass.ts(pid * 2 + hf, TH)]).then_inc(
                                   sems["s_rl"], 16)
            ST_R = 16 * (hf + 1)

            v_tp = {}
            n_tp = 0
            for c in range(HC):
                for tt in range(8):
                    @block.tensor
                    def _(t, c=c, tt=tt, n_tp=n_tp, fsv=fs.v):
                        if n_tp == 0:
                            t.wait_ge(sems["s_rl"], ST_R)
                            t.wait_ge(S_FFS, fsv)
                        if n_tp >= 2:
                            t.wait_ge(S_FFV, v_tp[n_tp - 2])
                        fm.inc(t.transpose(out=pst_of(n_tp),
                                           in_=r_fm[:, c * TH + tt * P:c * TH + (tt + 1) * P],
                                           identity=ident[:]))
                    @block.vector
                    def _(v, c=c, tt=tt, n_tp=n_tp, mt=fm.v):
                        v.wait_ge(S_FFM, mt)
                        fv.inc(v.tensor_copy(
                            out=x1_tm[:, tt * H + c * P:tt * H + (c + 1) * P],
                            in_=pst_of(n_tp)))
                    v_tp[n_tp] = fv.v
                    n_tp += 1
            FV_RTM = fv.v

            def emit_ln(buf, tt, first_wait):
                @block.vector
                def _(v, tt=tt, fw=first_wait):
                    if fw is not None:
                        v.wait_ge(S_FFV, fw)
                    fv.inc(v.tensor_tensor(out=sact[:], in0=buf[:, tt * H:(tt + 1) * H],
                                           in1=buf[:, tt * H:(tt + 1) * H], op=ALU.mult))
                    v.wait_ge(S_FFV, fv.v)
                    v.reduce_sum(out=stat[:, 0:1], in_=buf[:, tt * H:(tt + 1) * H], axis=AX.X)
                    fv.inc(v.reduce_sum(out=stat[:, 1:2], in_=sact[:], axis=AX.X))
                    v.wait_ge(S_FFV, fv.v)
                    v.tensor_scalar_mul(out=stat[:, 0:1], in0=stat[:, 0:1], scalar1=1.0 / H)
                    fv.inc(v.tensor_scalar_mul(out=stat[:, 1:2], in0=stat[:, 1:2], scalar1=1.0 / H))
                    v.wait_ge(S_FFV, fv.v)
                    fv.inc(v.tensor_tensor(out=stat[:, 2:3], in0=stat[:, 0:1],
                                           in1=stat[:, 0:1], op=ALU.mult))
                    v.wait_ge(S_FFV, fv.v)
                    fv.inc(v.tensor_tensor(out=stat[:, 1:2], in0=stat[:, 1:2],
                                           in1=stat[:, 2:3], op=ALU.subtract))
                    v.wait_ge(S_FFV, fv.v)
                    fv.inc(v.tensor_scalar_add(out=stat[:, 1:2], in0=stat[:, 1:2],
                                               scalar1=1e-5))
                    v.wait_ge(S_FFV, fv.v)
                    fv.inc(v.reciprocal(out=stat[:, 2:3], in_=stat[:, 1:2]))
                @block.scalar
                def _(s, vt=fv.v):
                    s.wait_ge(S_FFV, vt)
                    fs.inc(s.sqrt(out=stat[:, 3:4], in_=stat[:, 2:3]))
                @block.vector
                def _(v, tt=tt, st_=fs.v):
                    v.wait_ge(S_FFS, st_)
                    fv.inc(v.tensor_scalar(out=buf[:, tt * H:(tt + 1) * H],
                                           in0=buf[:, tt * H:(tt + 1) * H],
                                           scalar1=stat[:, 0:1], scalar2=stat[:, 3:4],
                                           op0=ALU.subtract, op1=ALU.mult))
                return fv.v

            for tt in range(8):
                emit_ln(x1_tm, tt, FV_RTM if tt == 0 else None)
            FV_LN1 = fv.v
            if _DEBUG_TAP == "x1":
                @block.sync
                def _(sync, hf=hf, vt=fv.v):
                    sync.wait_ge(S_FFV, vt)
                    st.inc(sync.dma_start(
                        out=dbg_h.rearrange("(tt p) hh -> p tt hh", p=P)[
                            :, hf * 8:(hf + 1) * 8, :],
                        in_=x1_tm.rearrange("p (tt hh) -> p tt hh", tt=8)))

            v_tp2 = {}
            n_tp2 = 0
            for c in range(HC):
                for tt in range(8):
                    @block.tensor
                    def _(t, c=c, tt=tt, n_tp2=n_tp2):
                        if n_tp2 == 0:
                            t.wait_ge(S_FFV, FV_LN1)
                        if n_tp2 >= 2:
                            t.wait_ge(S_FFV, v_tp2[n_tp2 - 2])
                        fm.inc(t.transpose(out=pst_of(n_tp2),
                                           in_=x1_tm[:, tt * H + c * P:tt * H + (c + 1) * P],
                                           identity=ident[:]))
                    @block.vector
                    def _(v, c=c, tt=tt, n_tp2=n_tp2, mt=fm.v):
                        v.wait_ge(S_FFM, mt)
                        fv.inc(v.tensor_copy(
                            out=x1_bf[:, c * TH + tt * P:c * TH + (tt + 1) * P],
                            in_=pst_of(n_tp2)))
                    v_tp2[n_tp2] = fv.v
                    n_tp2 += 1
            FV_X1BF = fv.v

            @block.tensor
            def _(t, yp=fv_yev_prev):
                t.wait_ge(S_FFV, FV_X1BF)
                t.wait_ge(S_FFV, FV_GW)
                if yp:
                    t.wait_ge(S_FFV, yp)
                last = None
                for c in range(HC):
                    for nk in range(TH // 512):
                        last = t.matmul(out=ps_y[0:E, nk * 512:(nk + 1) * 512],
                                        lhsT=gw_bf[:, c * E:(c + 1) * E],
                                        rhs=x1_bf[:, c * TH + nk * 512:c * TH + (nk + 1) * 512],
                                        start=(c == 0), stop=(c == HC - 1))
                fm.inc(last)
            @block.vector
            def _(v, mt=fm.v):
                v.wait_ge(S_FFM, mt)
                fv.inc(v.tensor_copy(out=lg_sb[:], in_=ps_y[0:E, :]))
            FV_LG = fv.v

            v_lgt = {}
            for tt in range(8):
                @block.tensor
                def _(t, tt=tt):
                    if tt == 0:
                        t.wait_ge(S_FFV, FV_LG)
                    if tt >= 2:
                        t.wait_ge(S_FFV, v_lgt[tt - 2])
                    fm.inc(t.transpose(out=pst_of(tt)[0:P, 0:E],
                                       in_=lg_sb[:, tt * P:(tt + 1) * P],
                                       identity=ident[0:E, 0:E]))
                @block.vector
                def _(v, tt=tt, mt=fm.v):
                    v.wait_ge(S_FFM, mt)
                    fv.inc(v.tensor_copy(out=lgtm[:, tt * E:(tt + 1) * E],
                                         in_=pst_of(tt)[0:P, 0:E]))
                v_lgt[tt] = fv.v

            for tt in range(8):
                @block.vector
                def _(v, tt=tt, vt=v_lgt[7]):
                    if tt == 0:
                        v.wait_ge(S_FFV, vt)
                    fv.inc(v.reduce_max(out=stat[:, 4 + tt:5 + tt],
                                        in_=lgtm[:, tt * E:(tt + 1) * E], axis=AX.X))
                    v.wait_ge(S_FFV, fv.v)
                    fv.inc(v.tensor_single_scalar(out=smtm[:, tt * E:(tt + 1) * E],
                                                  in_=lgtm[:, tt * E:(tt + 1) * E],
                                                  scalar=stat[:, 4 + tt:5 + tt],
                                                  op=ALU.is_equal))
            FV_EQ = fv.v
            @block.scalar
            def _(s, vt=FV_EQ):
                s.wait_ge(S_FFV, vt)
                fs.inc(s.activation(out=stat[:, 4:12], in_=stat[:, 4:12], func=AF.Sigmoid))
            FS_SIG = fs.v
            for tt in range(8):
                @block.vector
                def _(v, tt=tt, st_=FS_SIG):
                    if tt == 0:
                        v.wait_ge(S_FFS, st_)
                    fv.inc(v.tensor_scalar_mul(out=smtm[:, tt * E:(tt + 1) * E],
                                               in0=smtm[:, tt * E:(tt + 1) * E],
                                               scalar1=stat[:, 4 + tt:5 + tt]))
            FV_SM = fv.v

            v_smt = {}
            for tt in range(8):
                @block.tensor
                def _(t, tt=tt):
                    if tt == 0:
                        t.wait_ge(S_FFV, FV_SM)
                    if tt >= 2:
                        t.wait_ge(S_FFV, v_smt[tt - 2])
                    fm.inc(t.transpose(out=pst_of(tt)[0:E, 0:P],
                                       in_=smtm[:, tt * E:(tt + 1) * E],
                                       identity=ident[:]))
                @block.vector
                def _(v, tt=tt, mt=fm.v):
                    v.wait_ge(S_FFM, mt)
                    fv.inc(v.tensor_copy(out=sm_fm[:, tt * P:(tt + 1) * P],
                                         in_=pst_of(tt)[0:E, 0:P]))
                v_smt[tt] = fv.v
            FV_SMFM = fv.v

            for ei in range(9):
                if ei == 0:
                    gsrc, usrc, dsrc = shg_h, shu_h, shd_h
                    rhs_t = x1_bf
                else:
                    gsrc, usrc, dsrc = rtg_h[ei - 1], rtu_h[ei - 1], rtd_h[ei - 1]
                    rhs_t = xs_bf

                if ei >= 1:
                    e = ei - 1
                    @block.tensor
                    def _(t, e=e, sp=fs_sm_prev, lastv=fv.v):
                        t.wait_ge(S_FFV, FV_SMFM)
                        t.wait_ge(S_G, GG_SELB8)
                        if sp:
                            t.wait_ge(S_FFV, sp[1])
                        else:
                            t.wait_ge(S_FFV, lastv)
                        last = None
                        for nk in range(TH // 512):
                            last = t.matmul(out=ps_ty[:, nk * 512:(nk + 1) * 512],
                                            lhsT=selb8d[:, e * P:(e + 1) * P],
                                            rhs=sm_fm[:, nk * 512:(nk + 1) * 512],
                                            start=True, stop=True)
                        fm.inc(last)
                    @block.vector
                    def _(v, mt=fm.v):
                        v.wait_ge(S_FFM, mt)
                        fv.inc(v.tensor_copy(out=sm_bf[:], in_=ps_ty[:]))
                    fs_sm_prev = ("v", fv.v)
                    for c in range(HC):
                        @block.vector
                        def _(v, c=c, e=e, gup=fm_gu_prev):
                            if c == 0:
                                if gup:
                                    v.wait_ge(S_FFM, gup)
                            fv.inc(v.tensor_tensor(out=xs_bf[:, c * TH:(c + 1) * TH],
                                                   in0=x1_bf[:, c * TH:(c + 1) * TH],
                                                   in1=sm_bf[:], op=ALU.mult))
                FV_XS = fv.v

                for it in range(IT):
                    gstep = gu_step[0]
                    gu_step[0] += 1
                    g0, wt0 = emit_wload(gsrc[it], 1)
                    bg, (bgsem, bgv) = emit_wcast(g0, wt0, 1, "g")
                    g1, wt1 = emit_wload(usrc[it], 1)
                    bu, (ctsem, ctv) = emit_wcast(g1, wt1, 1, "g")
                    @block.tensor
                    def _(t, it=it, bg=bg, bu=bu, ctsem=ctsem, ctv=ctv,
                          bgsem=bgsem, bgv=bgv, xs_t=FV_XS, rhs_t=rhs_t, gstep=gstep):
                        t.wait_ge(bgsem, bgv)
                        t.wait_ge(ctsem, ctv)
                        t.wait_ge(S_FFV, xs_t)
                        for nk in range(2):
                            sw = fs_silu.get((gstep - 1, nk), 0)
                            aw = fv_act.get((gstep - 1, nk), 0)
                            if sw:
                                t.wait_ge(S_FFS, sw)
                            if aw:
                                t.wait_ge(S_FFV, aw)
                        last = None
                        for c in range(HC):
                            for nk in range(2):
                                last = t.matmul(out=ps_gu[nk][:],
                                                lhsT=wbf[:, bg * 1024 + c * P:bg * 1024 + (c + 1) * P],
                                                rhs=rhs_t[:, c * TH + nk * 512:c * TH + (nk + 1) * 512],
                                                start=(c == 0), stop=(c == HC - 1))
                        fm.inc(last)
                        for c in range(HC):
                            for nk in range(2):
                                last = t.matmul(out=ps_gu[2 + nk][:],
                                                lhsT=wbf[:, bu * 1024 + c * P:bu * 1024 + (c + 1) * P],
                                                rhs=rhs_t[:, c * TH + nk * 512:c * TH + (nk + 1) * 512],
                                                start=(c == 0), stop=(c == HC - 1))
                        fm.inc(last)
                    FM_G = fm.v - 1
                    FM_GU = fm.v
                    wbf_free[bg] = fm.v
                    wbf_free[bu] = fm.v
                    for nk in range(2):
                        @block.scalar
                        def _(s, it=it, nk=nk, mt=FM_G):
                            s.wait_ge(S_FFM, mt)
                            fs.inc(s.activation(out=sact[:, nk * 512:(nk + 1) * 512],
                                                in_=ps_gu[nk][:], func=AF.Silu))
                        fs_silu[(gstep, nk)] = fs.v
                        @block.vector
                        def _(v, it=it, nk=nk, st_=fs.v, mt=FM_GU):
                            v.wait_ge(S_FFS, st_)
                            v.wait_ge(S_FFM, mt)
                            fv.inc(v.tensor_tensor(
                                out=act_bf[:, it * TH + nk * 512:it * TH + (nk + 1) * 512],
                                in0=sact[:, nk * 512:(nk + 1) * 512],
                                in1=ps_gu[2 + nk][:], op=ALU.mult))
                        fv_act[(gstep, nk)] = fv.v
                fm_gu_prev = fm.v
                FV_ACT = fv.v

                for ht in range(HC):
                    gd, wtd = emit_wload(dsrc[ht], 2)
                    bd, (ctsem, ctv) = emit_wcast(gd, wtd, 2, "g")
                    @block.tensor
                    def _(t, ht=ht, bd=bd, ctsem=ctsem, ctv=ctv, at=FV_ACT, yp=fv_yev_prev):
                        t.wait_ge(ctsem, ctv)
                        t.wait_ge(S_FFV, at)
                        if yp:
                            t.wait_ge(S_FFV, yp)
                        last = None
                        for ic in range(IT):
                            for nk in range(2):
                                last = t.matmul(out=ps_y[:, nk * 512:(nk + 1) * 512],
                                                lhsT=wbf[:, bd * 1024 + ic * P:bd * 1024 + (ic + 1) * P],
                                                rhs=act_bf[:, ic * TH + nk * 512:ic * TH + (nk + 1) * 512],
                                                start=(ic == 0), stop=(ic == IT - 1))
                        fm.inc(last)
                    wbf_free[bd] = fm.v
                    wbf_free[bd + 1] = fm.v
                    @block.vector
                    def _(v, ht=ht, mt=fm.v, ei=ei):
                        v.wait_ge(S_FFM, mt)
                        if ei == 0:
                            fv.inc(v.tensor_copy(out=y_fm[:, ht * TH:(ht + 1) * TH],
                                                 in_=ps_y[:]))
                        else:
                            fv.inc(v.tensor_tensor(out=y_fm[:, ht * TH:(ht + 1) * TH],
                                                   in0=y_fm[:, ht * TH:(ht + 1) * TH],
                                                   in1=ps_y[:], op=ALU.add))
                    fv_yev_prev = fv.v
            FV_FFN = fv.v
            if _DEBUG_TAP == "yffn":
                @block.sync
                def _(sync, hf=hf, vt=fv.v):
                    sync.wait_ge(S_FFV, vt)
                    st.inc(sync.dma_start(
                        out=dbg_h.rearrange("(a p) f -> p a f", p=P)[:, hf * 8:(hf + 1) * 8, :],
                        in_=y_fm.rearrange("p (a f) -> p a f", a=8)))

            v_tp3 = {}
            n_tp3 = 0
            for c in range(HC):
                for tt in range(8):
                    @block.tensor
                    def _(t, c=c, tt=tt, n_tp3=n_tp3, fsv=fs.v):
                        if n_tp3 == 0:
                            t.wait_ge(S_FFV, FV_FFN)
                            t.wait_ge(S_FFS, fsv)
                        if n_tp3 >= 2:
                            t.wait_ge(S_FFV, v_tp3[n_tp3 - 2])
                        fm.inc(t.transpose(out=pst_of(n_tp3),
                                           in_=y_fm[:, c * TH + tt * P:c * TH + (tt + 1) * P],
                                           identity=ident[:]))
                    @block.vector
                    def _(v, c=c, tt=tt, n_tp3=n_tp3, mt=fm.v):
                        v.wait_ge(S_FFM, mt)
                        fv.inc(v.tensor_tensor(
                            out=r_fm[:, tt * H + c * P:tt * H + (c + 1) * P],
                            in0=x1_tm[:, tt * H + c * P:tt * H + (c + 1) * P],
                            in1=pst_of(n_tp3), op=ALU.add))
                    v_tp3[n_tp3] = fv.v
                    n_tp3 += 1
            FV_R2 = fv.v

            for tt in range(8):
                emit_ln(r_fm, tt, FV_R2 if tt == 0 else None)
                @block.sync
                def _(sync, tt=tt, hf=hf, vt=fv.v):
                    sync.wait_ge(S_FFV, vt)
                    st.inc(sync.dma_start(
                        out=out_h[hofs + tt * P:hofs + (tt + 1) * P, :],
                        in_=r_fm[:, tt * H:(tt + 1) * H]))
            ST_PREV_HALF = st.v

        stackD.close()

        @block.sync
        def _(sync):
            sync.wait_ge(S_ST, st.v)

    return nc


def _prep_inputs(ins):
    x = np.ascontiguousarray(np.asarray(ins["L0_attn_x"])[0])
    xT = np.ascontiguousarray(x.T)
    w1 = np.ascontiguousarray(
        np.concatenate([np.asarray(ins["L0_attn_x_proj_w"]).T,
                        np.asarray(ins["L0_attn_x_proj_b_w"]).T], axis=1))
    dtw_f = np.asarray(ins["L0_attn_dt_proj_w"])
    dtw_b = np.asarray(ins["L0_attn_dt_proj_b_w"])
    alog_f = np.asarray(ins["L0_attn_A_log"])
    alog_b = np.asarray(ins["L0_attn_A_b_log"])
    dtb_f = np.asarray(ins["L0_attn_dt_proj_b"])
    dtb_b = np.asarray(ins["L0_attn_dt_proj_b_b"])
    dv_f = np.asarray(ins["L0_attn_D"])
    dv_b = np.asarray(ins["L0_attn_D_b"])

    gw = np.ascontiguousarray(np.asarray(ins["L0_ffn_gate_w"]))
    shg = np.ascontiguousarray(np.asarray(ins["L0_ffn_sh_gate"])
                               .reshape(HC, P, IT, P).transpose(2, 1, 0, 3))
    shu = np.ascontiguousarray(np.asarray(ins["L0_ffn_sh_up"])
                               .reshape(HC, P, IT, P).transpose(2, 1, 0, 3))
    shd = np.ascontiguousarray(np.asarray(ins["L0_ffn_sh_down"])
                               .reshape(IT, P, HC, P).transpose(2, 1, 0, 3))
    rtg = np.ascontiguousarray(np.asarray(ins["L0_ffn_rt_gate"])
                               .reshape(E, HC, P, IT, P).transpose(0, 3, 2, 1, 4))
    rtu = np.ascontiguousarray(np.asarray(ins["L0_ffn_rt_up"])
                               .reshape(E, HC, P, IT, P).transpose(0, 3, 2, 1, 4))
    rtd = np.ascontiguousarray(np.asarray(ins["L0_ffn_rt_down"])
                               .reshape(E, IT, P, HC, P).transpose(0, 3, 2, 1, 4))

    in_maps = []
    for k in range(NCORES):
        sl = slice(P * k, P * (k + 1))
        tok = slice(TC * k, TC * (k + 1))
        in_maps.append({
            "xk_tok": np.ascontiguousarray(xT[:, tok]),
            "xk_str": np.ascontiguousarray(xT[sl, :]),
            "w1": w1,
            "dtw": np.ascontiguousarray(
                np.concatenate([dtw_f[sl].T, dtw_b[sl].T], axis=1)),
            "dtb": np.ascontiguousarray(np.stack([dtb_f[sl], dtb_b[sl]], axis=1)),
            "alog": np.ascontiguousarray(np.concatenate([alog_f[sl], alog_b[sl]], axis=1)),
            "dvec": np.ascontiguousarray(np.stack([dv_f[sl], dv_b[sl]], axis=1)),
            "gw": gw, "shg": shg, "shu": shu, "shd": shd,
            "rtg": rtg, "rtu": rtu, "rtd": rtd,
        })
    return in_maps


def kernel(**inputs):
    in_maps = _prep_inputs(inputs)
    nc = build_program(stage=2)
    res = run_bass_kernel_spmd(nc, in_maps, core_ids=list(range(NCORES)))
    out = np.concatenate([res.results[k]["out"] for k in range(NCORES)], axis=0)
    return out.reshape(1, T, H)
